# version 14
# baseline (speedup 1.0000x reference)
import contextlib
import numpy as np
import ml_dtypes
import concourse.bass as bass
import concourse.mybir as mybir
from concourse.bass_utils import run_bass_kernel_spmd

F32 = mybir.dt.float32
BF16 = mybir.dt.bfloat16
AF = mybir.ActivationFunctionType
ALU = mybir.AluOpType
AX = mybir.AxisListType

S = 2048
D = 4096
NTC = 16
NJC = 32
NQH = 32
NKV = 8
E = 64
NPT = 17

ENGS = ("tensor", "scalar", "vector", "sync", "gpsimd")
SWAP_MASK = [i ^ 1 for i in range(32)]


class Sem:
    __slots__ = ("h", "n")

    def __init__(self, handle=None):
        self.h = handle
        self.n = 0

    def inc(self, k=1):
        self.n += k
        return self.n


def _rope_tables():
    import jax
    import jax.numpy as jnp
    cpu = jax.devices("cpu")[0]
    with jax.default_device(cpu):
        freqs = 1.0 / 10000.0 ** (jnp.arange(0, 128, 2, dtype=jnp.float32) / 128)
        t = jnp.arange(S, dtype=jnp.float32)
        ang = jnp.outer(t, freqs)
        cos = np.asarray(jnp.cos(ang))
        sin = np.asarray(jnp.sin(ang))
    C = np.empty((128, S), np.float32)
    Sg = np.empty((128, S), np.float32)
    C[0::2] = cos.T
    C[1::2] = cos.T
    Sg[0::2] = -sin.T
    Sg[1::2] = sin.T
    return np.ascontiguousarray(C), np.ascontiguousarray(Sg)


DEBUG = False


def build_program():
    nc = bass.Bass("TRN2", target_bir_lowering=False, detect_race_conditions=False)

    x_in = nc.dram_tensor("x", [S, D], F32, kind="ExternalInput")
    wq = nc.dram_tensor("wq", [D, D], F32, kind="ExternalInput")
    wk = nc.dram_tensor("wk", [D, NKV * 128], F32, kind="ExternalInput")
    wv = nc.dram_tensor("wv", [D, NKV * 128], F32, kind="ExternalInput")
    wo = nc.dram_tensor("wo", [D, D], F32, kind="ExternalInput")
    rw = nc.dram_tensor("rw", [E, D], F32, kind="ExternalInput")
    rb_bc = nc.dram_tensor("rb_bc", [128, E], F32, kind="ExternalInput")
    eb = nc.dram_tensor("eb", [E, D], F32, kind="ExternalInput")
    ctab_in = nc.dram_tensor("ctab", [128, S], F32, kind="ExternalInput")
    stab_in = nc.dram_tensor("stab", [128, S], F32, kind="ExternalInput")
    identb_in = nc.dram_tensor("identb", [128, 128], BF16, kind="ExternalInput")
    identf_in = nc.dram_tensor("identf", [128, 128], F32, kind="ExternalInput")
    tri_in = nc.dram_tensor("tri", [128, 128], BF16, kind="ExternalInput")
    onescol_in = nc.dram_tensor("onescol", [128, 1], BF16, kind="ExternalInput")
    onesrow_in = nc.dram_tensor("onesrow", [1, 512], F32, kind="ExternalInput")
    c32_in = nc.dram_tensor("c32", [128, 1], F32, kind="ExternalInput")
    out = nc.dram_tensor("out", [S, D], F32, kind="ExternalOutput")
    scr_kind = "ExternalOutput" if DEBUG else "Internal"
    kt_scr = nc.dram_tensor("kt_scr", [NKV, 128, S], BF16, kind=scr_kind)
    v_scr = nc.dram_tensor("v_scr", [NKV, 128, S], BF16, kind=scr_kind)
    ont_scr = nc.dram_tensor("ont_scr", [NTC, 128, D], BF16, kind=scr_kind)
    b1_scr = nc.dram_tensor("b1_scr", [1, S], F32, kind=scr_kind)
    if DEBUG:
        dbg_xnT = nc.dram_tensor("dbg_xnT", [128, NJC, S], BF16, kind="ExternalOutput")
        dbg_wdT = nc.dram_tensor("dbg_wdT", [E, S], BF16, kind="ExternalOutput")
        dbg_st = nc.dram_tensor("dbg_st", [128, 4 * NTC], F32, kind="ExternalOutput")
        dbg_st2 = nc.dram_tensor("dbg_st2", [128, 2 * NTC], F32, kind="ExternalOutput")
        dbg_qt = nc.dram_tensor("dbg_qt", [NQH, 128, S], BF16, kind="ExternalOutput")

    with contextlib.ExitStack() as top:
        Et = top.enter_context
        sem_names = [
            "ld", "x1", "x2", "st", "ts", "vp", "ve", "rt", "wd", "wvl",
            "wql", "qm", "qe", "sw", "rp", "kvl", "se", "sm", "pc", "oz",
            "onv", "sp", "wol", "lhs", "xD", "dc", "od", "mr",
            "wc0", "wc1", "wc2", "wc3", "wc4", "wc5", "wc6", "wc7",
            "wqA", "wqB", "spA", "spB", "lhA", "lhB", "xdA", "xdB", "odA", "odB",
        ]
        SM = {n: Sem(Et(nc.semaphore(n))) for n in sem_names}
        SM["_tile"] = Sem()
        MK = {}

        def VH(eng, E_, ins):
            sw = SM["sw"]
            sw.inc()
            if eng == "vector":
                ins.then_inc(sw.h, 1)
                E_.wait_ge(sw.h, sw.n)

        def run_block(phase_fn):
            snap = {k: s.n for k, s in SM.items()}
            mk_snap = dict(MK)
            with nc.Block() as blk:
                for name in ENGS:
                    def walk(E_, name=name):
                        for k, s in SM.items():
                            s.n = snap[k]
                        MK.clear()
                        MK.update(mk_snap)
                        phase_fn(name, E_)
                    getattr(blk, name)(walk)

        c32_sb = Et(nc.sbuf_tensor("c32_sb", [128, 1], F32))
        rec = Et(nc.sbuf_tensor("rec", [128, 1], F32))
        wdT = Et(nc.sbuf_tensor("wdT", [E, S], BF16))
        rstd = Et(nc.sbuf_tensor("rstd", [128, NTC], F32))
        mrneg = Et(nc.sbuf_tensor("mrneg", [128, NTC], F32))
        wsum = Et(nc.sbuf_tensor("wsum", [128, NTC], F32))
        a1c = Et(nc.sbuf_tensor("a1c", [128, NTC], F32))
        t16a = Et(nc.sbuf_tensor("t16a", [128, NTC], F32))
        _xnT_cm = contextlib.ExitStack()
        xnT = _xnT_cm.enter_context(nc.sbuf_tensor("xnT", [128, NJC, S], BF16))

        with contextlib.ExitStack() as phab:
            Ea = phab.enter_context
            ident_f = Ea(nc.sbuf_tensor("ident_f", [128, 128], F32))
            ident_b = Ea(nc.sbuf_tensor("ident_b", [128, 128], BF16))
            rb_sb = Ea(nc.sbuf_tensor("rb_sb", [128, E], F32))
            x2h = Ea(nc.sbuf_tensor("x2h", [128, 2, 2048], F32))
            th_sb = Ea(nc.sbuf_tensor("th_sb", [128, 2048], F32))
            t16T = th_sb[0:16, 1536:1664]
            xn_bf = Ea(nc.sbuf_tensor("xn_bf", [128, 2048], BF16))
            rw_bf = Ea(nc.sbuf_tensor("rw_bf", [E, D], BF16))
            rwT = Ea(nc.sbuf_tensor("rwT", [128, NJC, E], BF16))
            wv_sb = Ea(nc.sbuf_tensor("wv_sb", [128, NJC, 512], BF16))
            v_ev = Ea(nc.sbuf_tensor("v_ev", [128, 2, 512], BF16))
            rz1 = Ea(nc.sbuf_tensor("rz1", [128, 1], F32))
            s1a = Ea(nc.sbuf_tensor("s1a", [128, 1], F32))
            s1b = Ea(nc.sbuf_tensor("s1b", [128, 1], F32))
            s1c = Ea(nc.sbuf_tensor("s1c", [128, 1], F32))
            lg_sb = th_sb[:, 1792:1856]
            esb = th_sb[:, 1856:1920]
            p_sb = th_sb[:, 1920:1984]
            nlmax = th_sb[:, 1984:1985]
            zz = th_sb[:, 1985:1986]
            tk8 = th_sb[:, 1986:1994]
            wd_bf = Ea(nc.sbuf_tensor("wdb_sb", [128, 2, 64], BF16))
            ps_tb = Ea(nc.psum_tensor("ps_tb", [128, 2, 1024], BF16))
            ps_v = Ea(nc.psum_tensor("ps_v", [128, 2, 512], F32))
            ps_lg = Ea(nc.psum_tensor("ps_lg", [128, 2, 512], F32))
            ps_z0 = Ea(nc.psum_tensor("ps_z0", [128, 512], F32))

            def phaseAB(eng, E_):
                ld, st, ts, vp, ve, rt, wd, wvl, dc = (
                    SM["ld"], SM["st"], SM["ts"], SM["vp"], SM["ve"], SM["rt"],
                    SM["wd"], SM["wvl"], SM["dc"])
                lhA, lhB = SM["lhA"], SM["lhB"]
                mr = SM["mr"]

                def VHOP(ins):
                    sw = SM["sw"]
                    sw.inc()
                    if eng == "vector":
                        ins.then_inc(sw.h, 1)
                        E_.wait_ge(sw.h, sw.n)

                for dst, srcd in [(ident_f, identf_in), (ident_b, identb_in),
                                  (c32_sb, c32_in), (rb_sb, rb_bc)]:
                    if eng == "sync":
                        E_.dma_start(dst[:, :], srcd[:, :]).then_inc(ld.h, 16)
                    ld.inc(16)
                ld_all = ld.n
                if eng == "gpsimd":
                    E_.dma_start(rw_bf[:, :], rw[:, :]).then_inc(dc.h, 16)
                dc.inc(16)
                rw_done = dc.n
                for c in range(8):
                    if eng == "gpsimd":
                        E_.dma_start(
                            wv_sb[:, c * 4:(c + 1) * 4, :],
                            wv[c * 512:(c + 1) * 512, 0:512]
                            .rearrange("(jg p) i -> p jg i", p=128),
                        ).then_inc(wvl.h, 16)
                    wvl.inc(16)
                wv_all = wvl.n
                if eng == "vector":
                    E_.wait_ge(ld.h, ld_all)
                    E_.reciprocal(rec[:, :], c32_sb[:, :])
                    E_.memset(ps_z0[:, :], 0.0)
                    E_.engine_nop().then_inc(ve.h, 1)
                ve.inc()
                z0_done = ve.n
                for j in range(NJC):
                    bk = j % 2
                    if eng == "tensor":
                        if j == 0:
                            E_.wait_ge(dc.h, rw_done)
                            E_.wait_ge(ld.h, ld_all)
                        if j >= 2:
                            E_.wait_ge(st.h, st.n - 1)
                        E_.transpose(ps_tb[:, bk, 0:64], rw_bf[0:64, j * 128:(j + 1) * 128],
                                     ident_b[0:64, 0:64]).then_inc(ts.h, 1)
                    ts.inc()
                    if eng == "scalar":
                        E_.wait_ge(ts.h, ts.n)
                        E_.copy(rwT[:, j, :], ps_tb[:, bk, 0:64]).then_inc(st.h, 1)
                    st.inc()
                def flush_vspill(ptc):
                    ppb = ptc % 2
                    _sps = SM["spA"] if ppb == 0 else SM["spB"]
                    if eng == "sync":
                        E_.wait_ge(st.h, MK[("vev", ptc)])
                        E_.dma_start(
                            v_scr[0:4, :, ptc * 128:(ptc + 1) * 128].transpose([1, 0, 2]),
                            v_ev[:, ppb, :].rearrange("p (g d) -> p g d", d=128),
                        ).then_inc(_sps.h, 16)
                    _sps.inc(16)
                    MK[("vsp", ptc)] = (_sps, _sps.n)

                def flush_vev(ptc):
                    ppb = ptc % 2
                    if eng == "scalar":
                        E_.wait_ge(ts.h, MK[("vmm", ptc)])
                        if MK.get(("vsp", ptc - 2)) is not None:
                            _s, _v = MK[("vsp", ptc - 2)]
                            E_.wait_ge(_s.h, _v)
                        E_.activation(v_ev[:, ppb, :], ps_v[:, ppb, :],
                                      AF.Copy).then_inc(st.h, 1)
                    st.inc()
                    MK[("vev", ptc)] = st.n
                    MK[("vspill_pend",)] = ptc

                for tc in range(NTC):
                    pb = tc % 2
                    if MK.get(("vspill_pend",)) is not None:
                        flush_vspill(MK.pop(("vspill_pend",)))
                    for hf, hsem in ((0, lhA), (1, lhB)):
                        if eng == "sync":
                            dep = MK.get(("xfree", hf))
                            if dep is not None:
                                E_.wait_ge(dep[0].h, dep[1])
                            E_.dma_start(
                                x2h[:, hf, :],
                                x_in[tc * 128:(tc + 1) * 128, hf * 2048:(hf + 1) * 2048],
                            ).then_inc(hsem.h, 16)
                        hsem.inc(16)
                        MK[("xload", hf)] = hsem.n
                    for hf, hsem, acc in ((0, lhA, s1a), (1, lhB, s1b)):
                        if eng == "scalar":
                            E_.wait_ge(hsem.h, MK[("xload", hf)])
                            E_.activation(th_sb[:, :], x2h[:, hf, :], AF.Square,
                                          accum_out=acc[:, :]).then_inc(rt.h, 1)
                        rt.inc()
                        MK[("sq", hf)] = rt.n
                        if eng == "vector":
                            E_.wait_ge(hsem.h, MK[("xload", hf)])
                            E_.reduce_sum(s1c[:, :] if hf == 0 else rz1[:, :],
                                          x2h[:, hf, :], AX.X).then_inc(ve.h, 1)
                        ve.inc()
                    if eng == "vector":
                        E_.wait_ge(rt.h, MK[("sq", 1)])
                    VHOP(E_.tensor_tensor(s1a[:, :], s1a[:, :], s1b[:, :], ALU.add) if eng == "vector" else None)
                    VHOP(E_.tensor_scalar(s1a[:, :], s1a[:, :], 1.0 / 4096, None, ALU.mult) if eng == "vector" else None)
                    VHOP(E_.tensor_tensor(s1c[:, :], s1c[:, :], rz1[:, :], ALU.add) if eng == "vector" else None)
                    VHOP(E_.tensor_scalar(s1c[:, :], s1c[:, :], 1.0 / 4096, None, ALU.mult) if eng == "vector" else None)
                    VHOP(E_.tensor_tensor(s1b[:, :], s1c[:, :], s1c[:, :], ALU.mult) if eng == "vector" else None)
                    VHOP(E_.tensor_tensor(s1a[:, :], s1a[:, :], s1b[:, :], ALU.subtract) if eng == "vector" else None)
                    if eng == "vector":
                        E_.tensor_scalar(s1a[:, :], s1a[:, :], 1e-5, None, ALU.add).then_inc(vp.h, 1)
                    vp.inc()
                    if eng == "scalar":
                        E_.wait_ge(vp.h, vp.n)
                        E_.activation(s1b[:, :], s1a[:, :], AF.Ln)
                        E_.activation(rz1[:, :], s1b[:, :], AF.Exp, scale=-0.5).then_inc(vp.h, 1)
                    vp.inc()
                    if eng == "vector":
                        E_.wait_ge(vp.h, vp.n)
                    for _ in range(2):
                        VHOP(E_.tensor_tensor(s1b[:, :], rz1[:, :], rz1[:, :], ALU.mult) if eng == "vector" else None)
                        VHOP(E_.tensor_tensor(s1b[:, :], s1b[:, :], s1a[:, :], ALU.mult) if eng == "vector" else None)
                        VHOP(E_.tensor_scalar(s1b[:, :], s1b[:, :], -0.5, 1.5, ALU.mult, ALU.add) if eng == "vector" else None)
                        VHOP(E_.tensor_tensor(rz1[:, :], rz1[:, :], s1b[:, :], ALU.mult) if eng == "vector" else None)
                    VHOP(E_.tensor_copy(rstd[:, tc:tc + 1], rz1[:, :]) if eng == "vector" else None)
                    VHOP(E_.tensor_tensor(s1c[:, :], s1c[:, :], rz1[:, :], ALU.mult) if eng == "vector" else None)
                    if eng == "vector":
                        E_.tensor_scalar(mrneg[:, tc:tc + 1], s1c[:, :], -1.0, None,
                                         ALU.mult).then_inc(ve.h, 1)
                    ve.inc()
                    stats_done = ve.n
                    for hf, hsem in ((0, lhA), (1, lhB)):
                        for sl in range(4):
                            if eng == "vector":
                                if hf == 0 and sl == 0:
                                    E_.wait_ge(ve.h, z0_done)
                                if hf == 1 and sl == 0:
                                    E_.wait_ge(ts.h, MK.get(("trA",), 0))
                                E_.scalar_tensor_tensor(
                                    th_sb[:, sl * 512:(sl + 1) * 512],
                                    x2h[:, hf, sl * 512:(sl + 1) * 512],
                                    mrneg[:, tc:tc + 1], ps_z0[:, :],
                                    ALU.add, ALU.add).then_inc(ve.h, 1)
                            ve.inc()
                            MK[("stt", hf, sl)] = ve.n
                            if eng == "scalar":
                                E_.wait_ge(ve.h, MK[("stt", hf, sl)])
                                if hf == 0 and sl == 0 and tc > 0:
                                    E_.wait_ge(ts.h, MK[("trB",)])
                                E_.activation(xn_bf[:, sl * 512:(sl + 1) * 512],
                                              th_sb[:, sl * 512:(sl + 1) * 512],
                                              AF.Copy, scale=rstd[:, tc:tc + 1]).then_inc(rt.h, 1)
                            rt.inc()
                        MK[("xnact", hf)] = rt.n
                        if hf == 1:
                            MK[("sttB",)] = ve.n
                        MK[("xfree", hf)] = (SM["ve"], MK[("stt", hf, 3)])
                        for jj in range(16):
                            j = hf * 16 + jj
                            bk = j % 2
                            if eng == "tensor":
                                if jj == 0:
                                    E_.wait_ge(rt.h, MK[("xnact", hf)])
                                E_.wait_ge(st.h, st.n - 1)
                                E_.transpose(ps_tb[:, bk, 0:128],
                                             xn_bf[:, jj * 128:(jj + 1) * 128],
                                             ident_b[:, :]).then_inc(ts.h, 1)
                            ts.inc()
                            if eng == "scalar":
                                E_.wait_ge(ts.h, ts.n)
                                E_.copy(xnT[:, j, tc * 128:(tc + 1) * 128],
                                        ps_tb[:, bk, 0:128]).then_inc(st.h, 1)
                            st.inc()
                        if hf == 0:
                            MK[("trA",)] = ts.n
                        else:
                            MK[("trB",)] = ts.n
                    MK[("xn_done",)] = st.n
                    if MK.get(("vev_pend",)) is not None:
                        flush_vev(MK.pop(("vev_pend",)))
                    if eng == "tensor":
                        if tc == 0:
                            E_.wait_ge(wvl.h, wv_all)
                        E_.wait_ge(st.h, MK[("xn_done",)])
                        if MK.get(("vev", tc - 2)) is not None:
                            E_.wait_ge(st.h, MK[("vev", tc - 2)])
                        for j in range(NJC):
                            mm = E_.matmul(ps_v[:, pb, :],
                                           xnT[:, j, tc * 128:(tc + 1) * 128],
                                           wv_sb[:, j, :],
                                           start=(j == 0), stop=(j == NJC - 1))
                        mm.then_inc(ts.h, 1)
                    ts.inc()
                    MK[("vmm", tc)] = ts.n
                    MK[("vev_pend",)] = tc
                if MK.get(("vspill_pend",)) is not None:
                    flush_vspill(MK.pop(("vspill_pend",)))
                if MK.get(("vev_pend",)) is not None:
                    flush_vev(MK.pop(("vev_pend",)))
                if MK.get(("vspill_pend",)) is not None:
                    flush_vspill(MK.pop(("vspill_pend",)))

            run_block(phaseAB)

            def phaseB2(eng, E_):
                wvl, ts, st, rt, wd, vp, mr = (SM["wvl"], SM["ts"], SM["st"],
                                               SM["rt"], SM["wd"], SM["vp"],
                                               SM["mr"])

                def VHOP(ins):
                    sw = SM["sw"]
                    sw.inc()
                    if eng == "vector":
                        ins.then_inc(sw.h, 1)
                        E_.wait_ge(sw.h, sw.n)

                for c in range(8):
                    if eng == "gpsimd":
                        E_.dma_start(
                            wv_sb[:, c * 4:(c + 1) * 4, :],
                            wv[c * 512:(c + 1) * 512, 512:1024]
                            .rearrange("(jg p) i -> p jg i", p=128),
                        ).then_inc(wvl.h, 16)
                    wvl.inc(16)
                wv_all = wvl.n

                def flush_wdtr(ptc):
                    if eng == "tensor":
                        E_.wait_ge(wd.h, MK[("wdn", ptc)])
                        E_.wait_ge(st.h, st.n - 1)
                        E_.transpose(ps_tb[0:64, (ptc + 1) % 2, 0:128],
                                     wd_bf[:, ptc % 2, :],
                                     ident_b[:, :]).then_inc(ts.h, 1)
                    ts.inc()
                    MK[("wdtr", ptc)] = ts.n
                    if eng == "scalar":
                        E_.wait_ge(ts.h, MK[("wdtr", ptc)])
                        E_.copy(wdT[:, ptc * 128:(ptc + 1) * 128],
                                ps_tb[0:64, (ptc + 1) % 2, 0:128]).then_inc(st.h, 1)
                    st.inc()

                for tc in range(NTC):
                    pb = tc % 2
                    if eng == "tensor":
                        if tc == 0:
                            E_.wait_ge(wvl.h, wv_all)
                        if MK.get(("b2ev", tc - 2)) is not None:
                            E_.wait_ge(st.h, MK[("b2ev", tc - 2)])
                        for j in range(NJC):
                            mm = E_.matmul(ps_v[:, pb, :],
                                           xnT[:, j, tc * 128:(tc + 1) * 128],
                                           wv_sb[:, j, :],
                                           start=(j == 0), stop=(j == NJC - 1))
                        mm.then_inc(ts.h, 1)
                    ts.inc()
                    mm_done = ts.n
                    if eng == "tensor":
                        if MK.get(("lgr", tc - 2)) is not None:
                            E_.wait_ge(rt.h, MK[("lgr", tc - 2)])
                        for j in range(NJC):
                            mm = E_.matmul(ps_lg[:, pb, 0:E],
                                           xnT[:, j, tc * 128:(tc + 1) * 128],
                                           rwT[:, j, :], start=(j == 0),
                                           stop=(j == NJC - 1))
                        mm.then_inc(ts.h, 1)
                    ts.inc()
                    lg_done = ts.n
                    if eng == "scalar":
                        E_.wait_ge(ts.h, mm_done)
                        if MK.get(("b2sp", tc - 2)) is not None:
                            _s, _v = MK[("b2sp", tc - 2)]
                            E_.wait_ge(_s.h, _v)
                        else:
                            _s, _v = MK[("vsp", NTC - 2 + tc)]
                            E_.wait_ge(_s.h, _v)
                        E_.activation(v_ev[:, pb, :], ps_v[:, pb, :], AF.Copy).then_inc(st.h, 1)
                    st.inc()
                    MK[("b2ev", tc)] = st.n
                    _sps = SM["spA"] if pb == 0 else SM["spB"]
                    if eng == "sync":
                        E_.wait_ge(st.h, MK[("b2ev", tc)])
                        E_.dma_start(
                            v_scr[4:8, :, tc * 128:(tc + 1) * 128].transpose([1, 0, 2]),
                            v_ev[:, pb, :].rearrange("p (g d) -> p g d", d=128),
                        ).then_inc(_sps.h, 16)
                    _sps.inc(16)
                    MK[("b2sp", tc)] = (_sps, _sps.n)
                    if MK.get(("wdtr_pend",)) is not None:
                        flush_wdtr(MK.pop(("wdtr_pend",)))
                    if eng == "vector":
                        E_.wait_ge(ts.h, lg_done)
                    VHOP(E_.tensor_tensor(lg_sb[:, :], ps_lg[:, pb, 0:E], rb_sb[:, :], ALU.add) if eng == "vector" else None)
                    if eng == "vector":
                        E_.reduce_max(nlmax[:, :], lg_sb[:, :], AX.X, negate=True).then_inc(rt.h, 1)
                    rt.inc()
                    MK[("lgr", tc)] = rt.n
                    if eng == "scalar":
                        E_.wait_ge(rt.h, rt.n)
                        E_.activation(esb[:, :], lg_sb[:, :], AF.Exp, bias=nlmax[:, :],
                                      scale=1.0).then_inc(rt.h, 1)
                    rt.inc()
                    if eng == "vector":
                        E_.wait_ge(rt.h, rt.n)
                    VHOP(E_.reduce_sum(zz[:, :], esb[:, :], AX.X) if eng == "vector" else None)
                    if eng == "vector":
                        E_.reciprocal(rz1[:, :], zz[:, :]).then_inc(rt.h, 1)
                    rt.inc()
                    if eng == "scalar":
                        E_.wait_ge(rt.h, rt.n)
                        E_.activation(p_sb[:, :], esb[:, :], AF.Copy, scale=rz1[:, :]).then_inc(rt.h, 1)
                    rt.inc()
                    if eng == "vector":
                        E_.wait_ge(rt.h, rt.n)
                        if MK.get(("wdtr", tc - 2)) is not None:
                            E_.wait_ge(ts.h, MK[("wdtr", tc - 2)])
                    VHOP(E_.max(tk8[:, :], p_sb[:, :]) if eng == "vector" else None)
                    VHOP(E_.reduce_sum(wsum[:, tc:tc + 1], tk8[:, :], AX.X) if eng == "vector" else None)
                    if eng == "vector":
                        E_.scalar_tensor_tensor(wd_bf[:, pb, :], p_sb[:, :], tk8[:, 7:8], p_sb[:, :],
                                                ALU.is_ge, ALU.mult).then_inc(wd.h, 1)
                    wd.inc()
                    MK[("wdn", tc)] = wd.n
                    MK[("wdtr_pend",)] = tc
                if MK.get(("wdtr_pend",)) is not None:
                    flush_wdtr(MK.pop(("wdtr_pend",)))
                if eng == "vector":
                    E_.wait_ge(st.h, st.n)
                VHOP(E_.tensor_tensor(a1c[:, :], rstd[:, :], wsum[:, :], ALU.mult) if eng == "vector" else None)
                VHOP(E_.tensor_scalar(a1c[:, :], a1c[:, :], 1.0, None, ALU.add) if eng == "vector" else None)
                if eng == "vector":
                    E_.tensor_tensor(t16a[:, :], mrneg[:, :], wsum[:, :], ALU.mult).then_inc(vp.h, 1)
                vp.inc()
                if eng == "tensor":
                    E_.wait_ge(vp.h, vp.n)
                    E_.transpose(ps_lg[0:16, 0, 0:128], t16a[:, :],
                                 ident_f[:, :]).then_inc(ts.h, 1)
                ts.inc()
                if eng == "scalar":
                    E_.wait_ge(ts.h, ts.n)
                    E_.activation(t16T[:, :], ps_lg[0:16, 0, 0:128], AF.Copy).then_inc(st.h, 1)
                st.inc()
                if eng == "sync":
                    E_.wait_ge(st.h, st.n)
                    E_.dma_start(b1_scr[0:1, :].rearrange("o (tc t) -> o tc t", t=128),
                                 t16T[:, :]).then_inc(mr.h, 16)
                mr.inc(16)

            run_block(phaseB2)

        with contextlib.ExitStack() as phc:
            Ec = phc.enter_context
            ctab = Ec(nc.sbuf_tensor("ctab_sb", [128, S], F32))
            stab = Ec(nc.sbuf_tensor("stab_sb", [128, S], F32))
            qres = Ec(nc.sbuf_tensor("qres", [128, 512], F32))
            qshuf = Ec(nc.sbuf_tensor("qshuf", [128, 512], F32))
            tri01 = Ec(nc.sbuf_tensor("tri01", [128, 128], BF16))
            ones_c = Ec(nc.sbuf_tensor("ones_c", [128, 1], BF16))
            ones_r = Ec(nc.sbuf_tensor("ones_rC", [1, 128], F32))
            w_ring = Ec(nc.sbuf_tensor("w_ring", [128, 2, NJC, 128], BF16))
            qt_sb = Ec(nc.sbuf_tensor("qt_sb", [128, S], BF16))
            kt_sb = Ec(nc.sbuf_tensor("kt_sb", [128, S], BF16))
            vg_sb = Ec(nc.sbuf_tensor("vg_sb", [128, NTC, 128], BF16))
            pt_sb = Ec(nc.sbuf_tensor("pt_sb", [128, NPT, 512], BF16))
            rz5 = Ec(nc.sbuf_tensor("rz5", [1, 512], F32))
            rzb = Ec(nc.sbuf_tensor("rzb", [128, 512], F32))
            on_ev = Ec(nc.sbuf_tensor("on_ev", [128, 2, 512], BF16))
            ps_q = Ec(nc.psum_tensor("ps_q", [128, 2, 512], F32))
            ps_s = Ec(nc.psum_tensor("ps_s", [128, 2, 512], F32))
            ps_o = Ec(nc.psum_tensor("ps_o", [128, 2, 512], F32))
            ps_z = Ec(nc.psum_tensor("ps_z", [128, 512], F32))
            ps_rz = Ec(nc.psum_tensor("ps_rz", [128, 512], F32))

            def emit_qk_head(eng, E_, w_dram, head, buf, scaled, dst_sb, mk_key):
                qm, qe, sw, rp = SM["qm"], SM["qe"], SM["sw"], SM["rp"]
                wqs = SM["wqA"] if buf == 0 else SM["wqB"]
                if eng == "gpsimd":
                    prev = MK.get(("wbuf", buf))
                    if prev is not None:
                        E_.wait_ge(qm.h, prev)
                    for c in range(8):
                        E_.dma_start(
                            w_ring[:, buf, c * 4:(c + 1) * 4, :],
                            w_dram[c * 512:(c + 1) * 512, head * 128:(head + 1) * 128]
                            .rearrange("(jg p) i -> p jg i", p=128),
                        ).then_inc(wqs.h, 16)
                wqs.inc(16 * 8)
                w_ready = wqs.n
                for q in range(4):
                    bk = q % 2
                    if eng == "tensor":
                        if q == 0:
                            E_.wait_ge(wqs.h, w_ready)
                        if MK.get(("qe_bank", bk)) is not None:
                            E_.wait_ge(qe.h, MK[("qe_bank", bk)])
                        for j in range(NJC):
                            mm = E_.matmul(ps_q[:, bk, :], w_ring[:, buf, j, :],
                                           xnT[:, j, q * 512:(q + 1) * 512],
                                           start=(j == 0), stop=(j == NJC - 1))
                        mm.then_inc(qm.h, 1)
                    qm.inc()
                    if eng == "scalar":
                        E_.wait_ge(qm.h, qm.n)
                        prev_rope = MK.get(("rope_last",))
                        if prev_rope is not None:
                            E_.wait_ge(rp.h, prev_rope)
                        if scaled:
                            E_.activation(qres[:, :], ps_q[:, bk, :], AF.Copy,
                                          scale=rec[:, :]).then_inc(qe.h, 1)
                        else:
                            E_.activation(qres[:, :], ps_q[:, bk, :], AF.Copy).then_inc(qe.h, 1)
                    qe.inc()
                    MK[("qe_bank", bk)] = qe.n
                    if eng == "vector":
                        E_.wait_ge(qe.h, qe.n)
                        dep = MK.get(("dst_free", mk_key))
                        if dep is not None and q == 0:
                            E_.wait_ge(dep[0], dep[1])
                        col0 = q * 512
                        E_.stream_shuffle(qshuf[:, :], qres[:, :], SWAP_MASK)
                        E_.tensor_tensor(qres[:, :], qres[:, :], ctab[:, col0:col0 + 512], ALU.mult)
                        E_.tensor_tensor(qshuf[:, :], qshuf[:, :], stab[:, col0:col0 + 512], ALU.mult)
                        E_.tensor_tensor(dst_sb[:, col0:col0 + 512], qres[:, :], qshuf[:, :],
                                         ALU.add).then_inc(rp.h, 1)
                    rp.inc()
                    MK[("rope_last",)] = rp.n
                    MK[(mk_key, q)] = rp.n
                MK[("wbuf", buf)] = qm.n

            def flush_bc_tail(eng, E_):
                pend = MK.get(("bc_pending",))
                if pend is None:
                    return
                MK[("bc_pending",)] = None
                ph, pqtb, pob, poz = pend
                ts, st, onv, oz = SM["ts"], SM["st"], SM["onv"], SM["oz"]
                if eng == "tensor":
                    E_.wait_ge(oz.h, poz)
                    if MK.get(("rzb_ev",)) is not None:
                        E_.wait_ge(st.h, MK[("rzb_ev",)])
                    E_.matmul(ps_rz[:, :], ones_r[0:1, 0:128], rz5[:, :],
                              start=True, stop=True).then_inc(ts.h, 1)
                ts.inc()
                bc_done = ts.n
                if eng == "scalar":
                    E_.wait_ge(ts.h, bc_done)
                    E_.activation(rzb[:, :], ps_rz[:, :], AF.Copy).then_inc(st.h, 1)
                st.inc()
                MK[("rzb_ev",)] = st.n
                if eng == "vector":
                    E_.wait_ge(st.h, MK[("rzb_ev",)])
                    dep = MK.get(("onv_spill", pqtb % 2))
                    if dep is not None:
                        E_.wait_ge(dep[0].h, dep[1])
                    E_.tensor_tensor(on_ev[:, pqtb % 2, :], ps_o[:, pob, :], rzb[:, :],
                                     ALU.mult).then_inc(onv.h, 1)
                onv.inc()
                MK[("onv_prev",)] = onv.n
                MK[("onv_bank", pob)] = onv.n
                _cs = SM["spA"] if pqtb % 2 == 0 else SM["spB"]
                if eng == "sync":
                    E_.wait_ge(onv.h, MK[("onv_prev",)])
                    E_.dma_start(
                        ont_scr[pqtb * 4:(pqtb + 1) * 4, :, ph * 128:(ph + 1) * 128]
                        .transpose([1, 0, 2]),
                        on_ev[:, pqtb % 2, :].rearrange("p (tcq tt) -> p tcq tt", tt=128),
                    ).then_inc(_cs.h, 16)
                _cs.inc(16)
                MK[("onv_spill", pqtb % 2)] = (_cs, _cs.n)

            def emit_attention(eng, E_, h):
                ts, se, sm, pc, oz, onv, sp, st = (
                    SM["ts"], SM["se"], SM["sm"], SM["pc"], SM["oz"], SM["onv"],
                    SM["sp"], SM["st"])
                tile_ctr = SM["_tile"]
                for qtb in range(4):
                    ob = (h * 4 + qtb) % 2
                    nk = 4 * (qtb + 1)
                    flush_bc_tail(eng, E_)
                    tiles = []
                    for ktc in range(nk):
                        goff = max(0, ktc * 128 - qtb * 512)
                        npr = 512 - goff
                        slot = tile_ctr.n % NPT
                        tno = tile_ctr.inc()
                        diag = (ktc >= qtb * 4)
                        if eng == "tensor":
                            if ktc == 0:
                                E_.wait_ge(SM["rp"].h, MK[("q", qtb)])
                            if MK.get(("se_tile", tno - 2)) is not None:
                                E_.wait_ge(se.h, MK[("se_tile", tno - 2)])
                            E_.matmul(ps_s[:, tno % 2, goff:goff + npr],
                                      kt_sb[:, ktc * 128:(ktc + 1) * 128],
                                      qt_sb[:, qtb * 512 + goff:(qtb + 1) * 512],
                                      start=True, stop=True).then_inc(ts.h, 1)
                        ts.inc()
                        MK[("ts_tile", tno)] = ts.n
                        if eng == "scalar":
                            E_.wait_ge(ts.h, MK[("ts_tile", tno)])
                            if MK.get(("pc_tile", tno - NPT)) is not None:
                                E_.wait_ge(pc.h, MK[("pc_tile", tno - NPT)])
                            E_.activation(pt_sb[:, slot, goff:goff + npr],
                                          ps_s[:, tno % 2, goff:goff + npr],
                                          AF.Exp).then_inc(se.h, 1)
                        se.inc()
                        MK[("se_tile", tno)] = se.n
                        if diag:
                            if eng == "vector":
                                E_.wait_ge(se.h, MK[("se_tile", tno)])
                                E_.tensor_tensor(pt_sb[:, slot, goff:goff + 128],
                                                 pt_sb[:, slot, goff:goff + 128],
                                                 tri01[:, :], ALU.mult).then_inc(sm.h, 1)
                            sm.inc()
                            MK[("sm_tile", tno)] = sm.n
                        tiles.append((ktc, goff, npr, slot, tno, diag))
                    for i, (ktc, goff, npr, slot, tno, diag) in enumerate(tiles):
                        if eng == "tensor":
                            if diag:
                                E_.wait_ge(sm.h, MK[("sm_tile", tno)])
                            else:
                                E_.wait_ge(se.h, MK[("se_tile", tno)])
                            if i == 0 and MK.get(("oz_prev",)) is not None:
                                E_.wait_ge(oz.h, MK[("oz_prev",)])
                            mm = E_.matmul(ps_z[0:1, goff:goff + npr], ones_c[:, :],
                                           pt_sb[:, slot, goff:goff + npr],
                                           start=(i == 0), stop=(i == nk - 1))
                            if i == nk - 1:
                                mm.then_inc(ts.h, 1)
                        if i == nk - 1:
                            ts.inc()
                    z_done = ts.n
                    if eng == "vector":
                        E_.wait_ge(ts.h, z_done)
                        if MK.get(("rzb_ev",)) is not None:
                            E_.wait_ge(st.h, MK[("rzb_ev",)])
                        E_.reciprocal(rz5[:, :], ps_z[0:1, :]).then_inc(oz.h, 1)
                    oz.inc()
                    MK[("oz_prev",)] = oz.n
                    for i, (ktc, goff, npr, slot, tno, diag) in enumerate(tiles):
                        if eng == "tensor":
                            if i == 0 and MK.get(("onv_bank", ob)) is not None:
                                E_.wait_ge(onv.h, MK[("onv_bank", ob)])
                            E_.matmul(ps_o[:, ob, goff:goff + npr], vg_sb[:, ktc, :],
                                      pt_sb[:, slot, goff:goff + npr],
                                      start=(i == 0), stop=(i == nk - 1)).then_inc(pc.h, 1)
                        pc.inc()
                        MK[("pc_tile", tno)] = pc.n
                    MK[("bc_pending",)] = (h, qtb, ob, oz.n)

            def phaseC(eng, E_):
                ld, kvl, sp, rp, ts, onv = (SM["ld"], SM["kvl"], SM["sp"], SM["rp"],
                                            SM["ts"], SM["onv"])
                for dst, srcd in [(ctab, ctab_in), (stab, stab_in), (tri01, tri_in),
                                  (ones_c, onescol_in), (ones_r, onesrow_in[0:1, 0:128])]:
                    if eng == "sync":
                        E_.dma_start(dst[:, :], srcd[:, :]).then_inc(ld.h, 16)
                    ld.inc(16)
                tabs = ld.n
                if eng == "vector":
                    E_.wait_ge(ld.h, tabs)
                for kh in range(NKV):
                    if kh > 0:
                        MK[("dst_free", "k")] = (sp.h, MK[("ksp", kh - 1)])
                    emit_qk_head(eng, E_, wk, kh, kh % 2, False, kt_sb, "k")
                    if eng == "sync":
                        E_.wait_ge(rp.h, MK[("k", 3)])
                        E_.dma_start(kt_scr[kh, :, :], kt_sb[:, :]).then_inc(sp.h, 16)
                    sp.inc(16)
                    MK[("ksp", kh)] = sp.n
                for h in range(NQH):
                    g = h // 4
                    if h % 4 == 0:
                        if eng == "sync":
                            if h > 0:
                                E_.wait_ge(ts.h, MK[("grp_ts",)])
                                E_.wait_ge(onv.h, MK[("onv_prev",)])
                                E_.wait_ge(SM["pc"].h, SM["pc"].n)
                            else:
                                E_.wait_ge(sp.h, MK[("ksp", NKV - 1)])
                            E_.dma_start(kt_sb[:, :], kt_scr[g, :, :]).then_inc(kvl.h, 16)
                            E_.dma_start(vg_sb[:, :, :],
                                         v_scr[g, :, :].rearrange("p (tcq tt) -> p tcq tt", tt=128),
                                         ).then_inc(kvl.h, 16)
                        kvl.inc(32)
                        MK[("kv_ready",)] = kvl.n
                    if h > 0:
                        MK[("dst_free", "q")] = (ts.h, MK[("qts_real",)])
                        if DEBUG:
                            MK[("dst_free", "q")] = (SM["od"].h, MK[("qdbg", h - 1)])
                    emit_qk_head(eng, E_, wq, h, h % 2, True, qt_sb, "q")
                    if DEBUG:
                        if eng == "sync":
                            E_.wait_ge(SM["rp"].h, MK[("q", 3)])
                            E_.dma_start(dbg_qt[h, :, :], qt_sb[:, :]).then_inc(SM["od"].h, 16)
                        SM["od"].inc(16)
                        MK[("qts_last",)] = None
                    if eng == "tensor":
                        E_.wait_ge(kvl.h, MK[("kv_ready",)])
                    emit_attention(eng, E_, h)
                    MK[("qts_real",)] = ts.n
                    MK[("qdbg", h)] = SM["od"].n
                    MK[("grp_ts",)] = ts.n
                flush_bc_tail(eng, E_)

            run_block(phaseC)
        _xnT_cm.close()

        with contextlib.ExitStack() as phd:
            Ed = phd.enter_context
            wo_sb = Ed(nc.sbuf_tensor("wo_sb", [128, NJC, 2048], BF16))
            eb_sb = Ed(nc.sbuf_tensor("eb_sb", [E, D], BF16))
            b1_sb = Ed(nc.sbuf_tensor("b1_sb", [1, S], F32))
            ones_r = Ed(nc.sbuf_tensor("ones_rD", [1, 512], F32))
            lhs_sb = Ed(nc.sbuf_tensor("lhs_sb", [128, 2, NJC, 128], BF16))
            xin_sb = Ed(nc.sbuf_tensor("xin_sb", [128, 2, 2048], F32))
            out_sb = Ed(nc.sbuf_tensor("out_sb", [128, 2, 2048], F32))
            ps_d = Ed(nc.psum_tensor("ps_d", [128, 2, 2048], F32))

            def phaseD(eng, E_):
                wol, lhs, xD, ts, onv, od, dc = (SM["wol"], SM["lhs"], SM["xD"],
                                                 SM["ts"], SM["onv"], SM["od"], SM["dc"])
                if eng == "gpsimd":
                    E_.dma_start(eb_sb[:, :], eb[:, :]).then_inc(dc.h, 16)
                dc.inc(16)
                if eng == "sync":
                    E_.dma_start(b1_sb[:, :], b1_scr[:, :]).then_inc(SM["mr"].h, 16)
                    E_.dma_start(ones_r[:, :], onesrow_in[:, :]).then_inc(SM["mr"].h, 16)
                SM["mr"].inc(32)
                dc_all = dc.n
                mr_all = SM["mr"].n
                for half in range(2):
                    wo_base = wol.n
                    for c in range(8):
                        if eng == "gpsimd":
                            if half == 1 and c == 0:
                                E_.wait_ge(ts.h, MK[("d_half_ts", 0)])
                            E_.dma_start(
                                wo_sb[:, c * 4:(c + 1) * 4, :],
                                wo[c * 512:(c + 1) * 512, half * 2048:(half + 1) * 2048]
                                .rearrange("(jg p) i -> p jg i", p=128),
                            ).then_inc(SM["wc%d" % c].h, 16)
                        SM["wc%d" % c].inc(16)
                    wo_ready = wol.n
                    for tc in range(NTC):
                        pb = tc % 2
                        gi = half * NTC + tc
                        _lh = SM["lhA"] if pb == 0 else SM["lhB"]
                        _xd = SM["xdA"] if pb == 0 else SM["xdB"]
                        _od = SM["odA"] if pb == 0 else SM["odB"]
                        if eng == "sync":
                            if MK.get(("d_mm", gi - 2)) is not None:
                                E_.wait_ge(ts.h, MK[("d_mm", gi - 2)])
                            if MK.get(("d_ev", gi - 2)) is not None:
                                E_.wait_ge(onv.h, MK[("d_ev", gi - 2)])
                            E_.dma_start(
                                lhs_sb[:, pb, :, :],
                                ont_scr[tc, :, :].rearrange("p (j i) -> p j i", i=128),
                            ).then_inc(_lh.h, 16)
                            E_.dma_start(
                                xin_sb[:, pb, :],
                                x_in[tc * 128:(tc + 1) * 128, half * 2048:(half + 1) * 2048],
                            ).then_inc(_xd.h, 16)
                        _lh.inc(16)
                        _xd.inc(16)
                        if eng == "tensor":
                            E_.wait_ge(_lh.h, _lh.n)
                            if tc == 0 and half == 0:
                                E_.wait_ge(dc.h, dc_all)
                                E_.wait_ge(SM["mr"].h, mr_all)
                            if MK.get(("d_ev", gi - 2)) is not None:
                                E_.wait_ge(onv.h, MK[("d_ev", gi - 2)])
                            for j in range(NJC):
                                if tc == 0 and j % 4 == 0:
                                    E_.wait_ge(SM["wc%d" % (j // 4)].h, 16 * (half + 1))
                                for c in range(4):
                                    E_.matmul(ps_d[:, pb, c * 512:(c + 1) * 512],
                                              lhs_sb[:, pb, j, :],
                                              wo_sb[:, j, c * 512:(c + 1) * 512],
                                              start=(j == 0), stop=False)
                            for c in range(4):
                                E_.matmul(ps_d[:, pb, c * 512:(c + 1) * 512],
                                          wdT[:, tc * 128:(tc + 1) * 128],
                                          eb_sb[:, half * 2048 + c * 512: half * 2048 + (c + 1) * 512],
                                          start=False, stop=False)
                            for c in range(4):
                                mm = E_.matmul(ps_d[:, pb, c * 512:(c + 1) * 512],
                                               b1_sb[0:1, tc * 128:(tc + 1) * 128], ones_r[:, :],
                                               start=False, stop=True)
                            mm.then_inc(ts.h, 1)
                        ts.inc()
                        MK[("d_mm", gi)] = ts.n
                        if eng == "vector":
                            E_.wait_ge(ts.h, MK[("d_mm", gi)])
                            E_.wait_ge(_xd.h, _xd.n)
                            if MK.get(("d_out", gi - 2)) is not None:
                                _o, _v = MK[("d_out", gi - 2)]
                                E_.wait_ge(_o.h, _v)
                            for c in range(4):
                                E_.scalar_tensor_tensor(
                                    out_sb[:, pb, c * 512:(c + 1) * 512],
                                    xin_sb[:, pb, c * 512:(c + 1) * 512],
                                    a1c[:, tc:tc + 1],
                                    ps_d[:, pb, c * 512:(c + 1) * 512],
                                    ALU.mult, ALU.add)
                            E_.engine_nop().then_inc(onv.h, 1)
                        onv.inc()
                        MK[("d_ev", gi)] = onv.n
                        if eng == "gpsimd":
                            E_.wait_ge(onv.h, MK[("d_ev", gi)])
                            E_.dma_start(
                                out[tc * 128:(tc + 1) * 128, half * 2048:(half + 1) * 2048],
                                out_sb[:, pb, :],
                            ).then_inc(_od.h, 16)
                        _od.inc(16)
                        MK[("d_out", gi)] = (_od, _od.n)
                    MK[("d_half_ts", half)] = ts.n
                if eng == "gpsimd":
                    E_.wait_ge(SM["odA"].h, SM["odA"].n)
                    E_.wait_ge(SM["odB"].h, SM["odB"].n)

            run_block(phaseD)

    return nc


def kernel(L0_attn_x, L0_attn_wq, L0_attn_wk, L0_attn_wv, L0_attn_wo,
           L0_ffn_router_weight, L0_ffn_router_bias, L0_ffn_expert_bias):
    C, Sg = _rope_tables()
    bf = ml_dtypes.bfloat16
    ident_b = np.eye(128, dtype=bf)
    ident_f = np.eye(128, dtype=np.float32)
    tri = np.triu(np.ones((128, 128), np.float32)).astype(bf)
    ones_col = np.ones((128, 1), bf)
    ones_row = np.ones((1, 512), np.float32)
    c32 = np.full((128, 1), np.float32(np.sqrt(np.float32(128.0))), np.float32)
    rb_bc = np.ascontiguousarray(np.broadcast_to(
        np.asarray(L0_ffn_router_bias, np.float32)[None, :], (128, E)))

    nc = build_program()
    x = np.ascontiguousarray(np.asarray(L0_attn_x, np.float32))
    common = {
        "wq": np.ascontiguousarray(L0_attn_wq), "wk": np.ascontiguousarray(L0_attn_wk),
        "wv": np.ascontiguousarray(L0_attn_wv), "wo": np.ascontiguousarray(L0_attn_wo),
        "rw": np.ascontiguousarray(L0_ffn_router_weight), "rb_bc": rb_bc,
        "eb": np.ascontiguousarray(L0_ffn_expert_bias),
        "ctab": C, "stab": Sg, "identb": ident_b, "identf": ident_f,
        "tri": tri, "onescol": ones_col, "onesrow": ones_row, "c32": c32,
    }
    in_maps = [dict(common, x=x[b]) for b in range(8)]
    res = run_bass_kernel_spmd(nc, in_maps, core_ids=list(range(8)))
    return np.stack([np.asarray(r["out"], np.float32) for r in res.results], axis=0)


# version 15
# speedup vs baseline: 1.1207x; 1.1207x over previous
import contextlib
import numpy as np
import ml_dtypes
import concourse.bass as bass
import concourse.mybir as mybir
from concourse.bass_utils import run_bass_kernel_spmd

F32 = mybir.dt.float32
BF16 = mybir.dt.bfloat16
AF = mybir.ActivationFunctionType
ALU = mybir.AluOpType
AX = mybir.AxisListType

S = 2048
D = 4096
NTC = 16
NJC = 32
NQH = 32
NKV = 8
E = 64
NPT = 17

ENGS = ("tensor", "scalar", "vector", "sync", "gpsimd")
SWAP_MASK = [i ^ 1 for i in range(32)]


class Sem:
    __slots__ = ("h", "n")

    def __init__(self, handle=None):
        self.h = handle
        self.n = 0

    def inc(self, k=1):
        self.n += k
        return self.n


def _rope_tables():
    import jax
    import jax.numpy as jnp
    cpu = jax.devices("cpu")[0]
    with jax.default_device(cpu):
        freqs = 1.0 / 10000.0 ** (jnp.arange(0, 128, 2, dtype=jnp.float32) / 128)
        t = jnp.arange(S, dtype=jnp.float32)
        ang = jnp.outer(t, freqs)
        cos = np.asarray(jnp.cos(ang))
        sin = np.asarray(jnp.sin(ang))
    C = np.empty((128, S), np.float32)
    Sg = np.empty((128, S), np.float32)
    C[0::2] = cos.T
    C[1::2] = cos.T
    Sg[0::2] = -sin.T
    Sg[1::2] = sin.T
    return np.ascontiguousarray(C), np.ascontiguousarray(Sg)


DEBUG = False


def build_program():
    nc = bass.Bass("TRN2", target_bir_lowering=False, detect_race_conditions=False)

    x_in = nc.dram_tensor("x", [S, D], F32, kind="ExternalInput")
    wq = nc.dram_tensor("wq", [D, D], F32, kind="ExternalInput")
    wk = nc.dram_tensor("wk", [D, NKV * 128], F32, kind="ExternalInput")
    wv = nc.dram_tensor("wv", [D, NKV * 128], F32, kind="ExternalInput")
    wo = nc.dram_tensor("wo", [D, D], F32, kind="ExternalInput")
    rw = nc.dram_tensor("rw", [E, D], F32, kind="ExternalInput")
    rb_bc = nc.dram_tensor("rb_bc", [128, E], F32, kind="ExternalInput")
    eb = nc.dram_tensor("eb", [E, D], F32, kind="ExternalInput")
    ctab_in = nc.dram_tensor("ctab", [128, S], F32, kind="ExternalInput")
    stab_in = nc.dram_tensor("stab", [128, S], F32, kind="ExternalInput")
    identb_in = nc.dram_tensor("identb", [128, 128], BF16, kind="ExternalInput")
    identf_in = nc.dram_tensor("identf", [128, 128], F32, kind="ExternalInput")
    tri_in = nc.dram_tensor("tri", [128, 128], BF16, kind="ExternalInput")
    onescol_in = nc.dram_tensor("onescol", [128, 1], BF16, kind="ExternalInput")
    onesrow_in = nc.dram_tensor("onesrow", [1, 512], F32, kind="ExternalInput")
    c32_in = nc.dram_tensor("c32", [128, 1], F32, kind="ExternalInput")
    out = nc.dram_tensor("out", [S, D], F32, kind="ExternalOutput")
    scr_kind = "ExternalOutput" if DEBUG else "Internal"
    kt_scr = nc.dram_tensor("kt_scr", [NKV, 128, S], BF16, kind=scr_kind)
    v_scr = nc.dram_tensor("v_scr", [NKV, 128, S], BF16, kind=scr_kind)
    ont_scr = nc.dram_tensor("ont_scr", [NTC, 128, D], BF16, kind=scr_kind)
    b1_scr = nc.dram_tensor("b1_scr", [1, S], F32, kind=scr_kind)
    if DEBUG:
        dbg_xnT = nc.dram_tensor("dbg_xnT", [128, NJC, S], BF16, kind="ExternalOutput")
        dbg_wdT = nc.dram_tensor("dbg_wdT", [E, S], BF16, kind="ExternalOutput")
        dbg_st = nc.dram_tensor("dbg_st", [128, 4 * NTC], F32, kind="ExternalOutput")
        dbg_st2 = nc.dram_tensor("dbg_st2", [128, 2 * NTC], F32, kind="ExternalOutput")
        dbg_qt = nc.dram_tensor("dbg_qt", [NQH, 128, S], BF16, kind="ExternalOutput")

    with contextlib.ExitStack() as top:
        Et = top.enter_context
        sem_names = [
            "ld", "x1", "x2", "st", "ts", "vp", "ve", "rt", "wd", "wvl",
            "wql", "qm", "qe", "sw", "rp", "kvl", "se", "sm", "pc", "oz",
            "onv", "sp", "wol", "lhs", "xD", "dc", "od", "mr",
            "wc0", "wc1", "wc2", "wc3", "wc4", "wc5", "wc6", "wc7",
            "wqA", "wqB", "spA", "spB", "lhA", "lhB", "xdA", "xdB", "odA", "odB",
            "vsA", "vsB",
        ]
        SM = {n: Sem(Et(nc.semaphore(n))) for n in sem_names}
        SM["_tile"] = Sem()
        MK = {}

        def VH(eng, E_, ins):
            sw = SM["sw"]
            sw.inc()
            if eng == "vector":
                ins.then_inc(sw.h, 1)
                E_.wait_ge(sw.h, sw.n)

        def run_block(phase_fn):
            snap = {k: s.n for k, s in SM.items()}
            mk_snap = dict(MK)
            with nc.Block() as blk:
                for name in ENGS:
                    def walk(E_, name=name):
                        for k, s in SM.items():
                            s.n = snap[k]
                        MK.clear()
                        MK.update(mk_snap)
                        phase_fn(name, E_)
                    getattr(blk, name)(walk)

        c32_sb = Et(nc.sbuf_tensor("c32_sb", [128, 1], F32))
        rec = Et(nc.sbuf_tensor("rec", [128, 1], F32))
        wdT = Et(nc.sbuf_tensor("wdT", [E, S], BF16))
        rstd = Et(nc.sbuf_tensor("rstd", [128, NTC], F32))
        mrneg = Et(nc.sbuf_tensor("mrneg", [128, NTC], F32))
        wsum = Et(nc.sbuf_tensor("wsum", [128, NTC], F32))
        a1c = Et(nc.sbuf_tensor("a1c", [128, NTC], F32))
        t16a = Et(nc.sbuf_tensor("t16a", [128, NTC], F32))
        _xnT_cm = contextlib.ExitStack()
        xnT = _xnT_cm.enter_context(nc.sbuf_tensor("xnT", [128, NJC, S], BF16))

        with contextlib.ExitStack() as phab:
            Ea = phab.enter_context
            ident_f = Ea(nc.sbuf_tensor("ident_f", [128, 128], F32))
            ident_b = Ea(nc.sbuf_tensor("ident_b", [128, 128], BF16))
            rb_sb = Ea(nc.sbuf_tensor("rb_sb", [128, E], F32))
            x2h = Ea(nc.sbuf_tensor("x2h", [128, 2, 2048], F32))
            th_sb = Ea(nc.sbuf_tensor("th_sb", [128, 2048], F32))
            t16T = th_sb[0:16, 1536:1664]
            xn_bf = Ea(nc.sbuf_tensor("xn_bf", [128, 2048], BF16))
            rw_bf = Ea(nc.sbuf_tensor("rw_bf", [E, D], BF16))
            rwT = Ea(nc.sbuf_tensor("rwT", [128, NJC, E], BF16))
            wv_sb = Ea(nc.sbuf_tensor("wv_sb", [128, NJC, 512], BF16))
            v_ev = Ea(nc.sbuf_tensor("v_ev", [128, 2, 512], BF16))
            rz1 = Ea(nc.sbuf_tensor("rz1", [128, 1], F32))
            s1a = Ea(nc.sbuf_tensor("s1a", [128, 1], F32))
            s1b = Ea(nc.sbuf_tensor("s1b", [128, 1], F32))
            s1c = Ea(nc.sbuf_tensor("s1c", [128, 1], F32))
            lg_sb = th_sb[:, 1792:1856]
            esb = th_sb[:, 1856:1920]
            p_sb = th_sb[:, 1920:1984]
            nlmax = th_sb[:, 1984:1985]
            zz = th_sb[:, 1985:1986]
            tk8 = th_sb[:, 1986:1994]
            wd_bf = Ea(nc.sbuf_tensor("wdb_sb", [128, 2, 64], BF16))
            ps_tb = Ea(nc.psum_tensor("ps_tb", [128, 2, 1024], BF16))
            ps_v = Ea(nc.psum_tensor("ps_v", [128, 2, 512], F32))
            ps_lg = Ea(nc.psum_tensor("ps_lg", [128, 2, 512], F32))
            ps_z0 = Ea(nc.psum_tensor("ps_z0", [128, 512], F32))

            def phaseAB(eng, E_):
                ld, st, ts, vp, ve, rt, wd, wvl, dc = (
                    SM["ld"], SM["st"], SM["ts"], SM["vp"], SM["ve"], SM["rt"],
                    SM["wd"], SM["wvl"], SM["dc"])
                lhA, lhB = SM["lhA"], SM["lhB"]
                mr = SM["mr"]

                def VHOP(ins):
                    sw = SM["sw"]
                    sw.inc()
                    if eng == "vector":
                        ins.then_inc(sw.h, 1)
                        E_.wait_ge(sw.h, sw.n)

                for dst, srcd in [(ident_f, identf_in), (ident_b, identb_in),
                                  (c32_sb, c32_in), (rb_sb, rb_bc)]:
                    if eng == "sync":
                        E_.dma_start(dst[:, :], srcd[:, :]).then_inc(ld.h, 16)
                    ld.inc(16)
                ld_all = ld.n
                if eng == "gpsimd":
                    E_.dma_start(rw_bf[:, :], rw[:, :]).then_inc(dc.h, 16)
                dc.inc(16)
                rw_done = dc.n
                for c in range(8):
                    if eng == "gpsimd":
                        E_.dma_start(
                            wv_sb[:, c * 4:(c + 1) * 4, :],
                            wv[c * 512:(c + 1) * 512, 0:512]
                            .rearrange("(jg p) i -> p jg i", p=128),
                        ).then_inc(wvl.h, 16)
                    wvl.inc(16)
                wv_all = wvl.n
                if eng == "vector":
                    E_.wait_ge(ld.h, ld_all)
                    E_.reciprocal(rec[:, :], c32_sb[:, :])
                    E_.memset(ps_z0[:, :], 0.0)
                    E_.engine_nop().then_inc(ve.h, 1)
                ve.inc()
                z0_done = ve.n
                for j in range(NJC):
                    bk = j % 2
                    if eng == "tensor":
                        if j == 0:
                            E_.wait_ge(dc.h, rw_done)
                            E_.wait_ge(ld.h, ld_all)
                        if j >= 2:
                            E_.wait_ge(st.h, st.n - 1)
                        E_.transpose(ps_tb[:, bk, 0:64], rw_bf[0:64, j * 128:(j + 1) * 128],
                                     ident_b[0:64, 0:64]).then_inc(ts.h, 1)
                    ts.inc()
                    if eng == "scalar":
                        E_.wait_ge(ts.h, ts.n)
                        E_.copy(rwT[:, j, :], ps_tb[:, bk, 0:64]).then_inc(st.h, 1)
                    st.inc()
                def flush_vev(ptc):
                    ppb = ptc % 2
                    _sps = SM["vsA"] if ppb == 0 else SM["vsB"]
                    if eng == "scalar":
                        E_.wait_ge(ts.h, MK[("vmm", ptc)])
                        if MK.get(("vsp", ptc - 2)) is not None:
                            _s, _v = MK[("vsp", ptc - 2)]
                            E_.wait_ge(_s.h, _v)
                        E_.activation(v_ev[:, ppb, :], ps_v[:, ppb, :],
                                      AF.Copy).then_inc(st.h, 1)
                    st.inc()
                    MK[("vev", ptc)] = st.n
                    if eng == "scalar":
                        E_.wait_ge(st.h, MK[("vev", ptc)])
                        E_.dma_start(
                            v_scr[0:4, :, ptc * 128:(ptc + 1) * 128].transpose([1, 0, 2]),
                            v_ev[:, ppb, :].rearrange("p (g d) -> p g d", d=128),
                        ).then_inc(_sps.h, 16)
                    _sps.inc(16)
                    MK[("vsp", ptc)] = (_sps, _sps.n)

                for tc in range(NTC):
                    pb = tc % 2
                    for hf, hsem in ((0, lhA), (1, lhB)):
                        if eng == "sync":
                            dep = MK.get(("xfree", hf))
                            if dep is not None:
                                E_.wait_ge(dep[0].h, dep[1])
                            E_.dma_start(
                                x2h[:, hf, :],
                                x_in[tc * 128:(tc + 1) * 128, hf * 2048:(hf + 1) * 2048],
                            ).then_inc(hsem.h, 16)
                        hsem.inc(16)
                        MK[("xload", hf)] = hsem.n
                    for hf, hsem, acc in ((0, lhA, s1a), (1, lhB, s1b)):
                        if eng == "scalar":
                            E_.wait_ge(hsem.h, MK[("xload", hf)])
                            E_.activation(th_sb[:, :], x2h[:, hf, :], AF.Square,
                                          accum_out=acc[:, :]).then_inc(rt.h, 1)
                        rt.inc()
                        MK[("sq", hf)] = rt.n
                        if eng == "vector":
                            E_.wait_ge(hsem.h, MK[("xload", hf)])
                            E_.reduce_sum(s1c[:, :] if hf == 0 else rz1[:, :],
                                          x2h[:, hf, :], AX.X).then_inc(ve.h, 1)
                        ve.inc()
                    if eng == "vector":
                        E_.wait_ge(rt.h, MK[("sq", 1)])
                    VHOP(E_.tensor_tensor(s1a[:, :], s1a[:, :], s1b[:, :], ALU.add) if eng == "vector" else None)
                    VHOP(E_.tensor_scalar(s1a[:, :], s1a[:, :], 1.0 / 4096, None, ALU.mult) if eng == "vector" else None)
                    VHOP(E_.tensor_tensor(s1c[:, :], s1c[:, :], rz1[:, :], ALU.add) if eng == "vector" else None)
                    VHOP(E_.tensor_scalar(s1c[:, :], s1c[:, :], 1.0 / 4096, None, ALU.mult) if eng == "vector" else None)
                    VHOP(E_.tensor_tensor(s1b[:, :], s1c[:, :], s1c[:, :], ALU.mult) if eng == "vector" else None)
                    VHOP(E_.tensor_tensor(s1a[:, :], s1a[:, :], s1b[:, :], ALU.subtract) if eng == "vector" else None)
                    if eng == "vector":
                        E_.tensor_scalar(s1a[:, :], s1a[:, :], 1e-5, None, ALU.add).then_inc(vp.h, 1)
                    vp.inc()
                    if eng == "scalar":
                        E_.wait_ge(vp.h, vp.n)
                        E_.activation(s1b[:, :], s1a[:, :], AF.Ln)
                        E_.activation(rz1[:, :], s1b[:, :], AF.Exp, scale=-0.5).then_inc(vp.h, 1)
                    vp.inc()
                    if eng == "vector":
                        E_.wait_ge(vp.h, vp.n)
                    for _ in range(2):
                        VHOP(E_.tensor_tensor(s1b[:, :], rz1[:, :], rz1[:, :], ALU.mult) if eng == "vector" else None)
                        VHOP(E_.tensor_tensor(s1b[:, :], s1b[:, :], s1a[:, :], ALU.mult) if eng == "vector" else None)
                        VHOP(E_.tensor_scalar(s1b[:, :], s1b[:, :], -0.5, 1.5, ALU.mult, ALU.add) if eng == "vector" else None)
                        VHOP(E_.tensor_tensor(rz1[:, :], rz1[:, :], s1b[:, :], ALU.mult) if eng == "vector" else None)
                    VHOP(E_.tensor_copy(rstd[:, tc:tc + 1], rz1[:, :]) if eng == "vector" else None)
                    VHOP(E_.tensor_tensor(s1c[:, :], s1c[:, :], rz1[:, :], ALU.mult) if eng == "vector" else None)
                    if eng == "vector":
                        E_.tensor_scalar(mrneg[:, tc:tc + 1], s1c[:, :], -1.0, None,
                                         ALU.mult).then_inc(ve.h, 1)
                    ve.inc()
                    stats_done = ve.n
                    for hf, hsem in ((0, lhA), (1, lhB)):
                        for sl in range(4):
                            if eng == "vector":
                                if hf == 0 and sl == 0:
                                    E_.wait_ge(ve.h, z0_done)
                                if hf == 1 and sl == 0:
                                    E_.wait_ge(ts.h, MK.get(("trA",), 0))
                                E_.scalar_tensor_tensor(
                                    th_sb[:, sl * 512:(sl + 1) * 512],
                                    x2h[:, hf, sl * 512:(sl + 1) * 512],
                                    mrneg[:, tc:tc + 1], ps_z0[:, :],
                                    ALU.add, ALU.add).then_inc(ve.h, 1)
                            ve.inc()
                            MK[("stt", hf, sl)] = ve.n
                            if eng == "scalar":
                                E_.wait_ge(ve.h, MK[("stt", hf, sl)])
                                if hf == 0 and sl == 0 and tc > 0:
                                    E_.wait_ge(ts.h, MK[("trB",)])
                                E_.activation(xn_bf[:, sl * 512:(sl + 1) * 512],
                                              th_sb[:, sl * 512:(sl + 1) * 512],
                                              AF.Copy, scale=rstd[:, tc:tc + 1]).then_inc(rt.h, 1)
                            rt.inc()
                        MK[("xnact", hf)] = rt.n
                        if hf == 1:
                            MK[("sttB",)] = ve.n
                        MK[("xfree", hf)] = (SM["ve"], MK[("stt", hf, 3)])
                        for g in range(4):
                            bk = g % 2
                            if eng == "tensor":
                                if g == 0:
                                    E_.wait_ge(rt.h, MK[("xnact", hf)])
                                E_.wait_ge(st.h, st.n - 1)
                                for q in range(4):
                                    jj = g * 4 + q
                                    mm = E_.transpose(ps_tb[:, bk, q * 128:(q + 1) * 128],
                                                      xn_bf[:, jj * 128:(jj + 1) * 128],
                                                      ident_b[:, :])
                                mm.then_inc(ts.h, 1)
                            ts.inc()
                            if eng == "scalar":
                                E_.wait_ge(ts.h, ts.n)
                                E_.copy(xnT[:, hf * 16 + g * 4:hf * 16 + (g + 1) * 4,
                                            tc * 128:(tc + 1) * 128],
                                        ps_tb[:, bk, 0:512].rearrange("p (b c) -> p b c", c=128)
                                        ).then_inc(st.h, 1)
                            st.inc()
                        if hf == 0:
                            MK[("trA",)] = ts.n
                        else:
                            MK[("trB",)] = ts.n
                    MK[("xn_done",)] = st.n
                    if MK.get(("vev_pend",)) is not None:
                        flush_vev(MK.pop(("vev_pend",)))
                    if eng == "tensor":
                        if tc == 0:
                            E_.wait_ge(wvl.h, wv_all)
                        E_.wait_ge(st.h, MK[("xn_done",)])
                        if MK.get(("vev", tc - 2)) is not None:
                            E_.wait_ge(st.h, MK[("vev", tc - 2)])
                        for j in range(NJC):
                            mm = E_.matmul(ps_v[:, pb, :],
                                           xnT[:, j, tc * 128:(tc + 1) * 128],
                                           wv_sb[:, j, :],
                                           start=(j == 0), stop=(j == NJC - 1))
                        mm.then_inc(ts.h, 1)
                    ts.inc()
                    MK[("vmm", tc)] = ts.n
                    MK[("vev_pend",)] = tc
                if MK.get(("vev_pend",)) is not None:
                    flush_vev(MK.pop(("vev_pend",)))
                MK[("vsp_fin",)] = [(SM["vsA"], SM["vsA"].n),
                                    (SM["vsB"], SM["vsB"].n)]

            run_block(phaseAB)

            def phaseB2(eng, E_):
                wvl, ts, st, rt, wd, vp, mr = (SM["wvl"], SM["ts"], SM["st"],
                                               SM["rt"], SM["wd"], SM["vp"],
                                               SM["mr"])

                def VHOP(ins):
                    sw = SM["sw"]
                    sw.inc()
                    if eng == "vector":
                        ins.then_inc(sw.h, 1)
                        E_.wait_ge(sw.h, sw.n)

                for c in range(8):
                    if eng == "gpsimd":
                        E_.dma_start(
                            wv_sb[:, c * 4:(c + 1) * 4, :],
                            wv[c * 512:(c + 1) * 512, 512:1024]
                            .rearrange("(jg p) i -> p jg i", p=128),
                        ).then_inc(wvl.h, 16)
                    wvl.inc(16)
                wv_all = wvl.n

                def flush_wdtr(ptc):
                    if eng == "tensor":
                        E_.wait_ge(wd.h, MK[("wdn", ptc)])
                        E_.wait_ge(st.h, st.n - 1)
                        E_.transpose(ps_tb[0:64, (ptc + 1) % 2, 0:128],
                                     wd_bf[:, ptc % 2, :],
                                     ident_b[:, :]).then_inc(ts.h, 1)
                    ts.inc()
                    MK[("wdtr", ptc)] = ts.n
                    if eng == "scalar":
                        E_.wait_ge(ts.h, MK[("wdtr", ptc)])
                        E_.copy(wdT[:, ptc * 128:(ptc + 1) * 128],
                                ps_tb[0:64, (ptc + 1) % 2, 0:128]).then_inc(st.h, 1)
                    st.inc()

                for tc in range(NTC):
                    pb = tc % 2
                    if eng == "tensor":
                        if tc == 0:
                            E_.wait_ge(wvl.h, wv_all)
                        if MK.get(("b2ev", tc - 2)) is not None:
                            E_.wait_ge(st.h, MK[("b2ev", tc - 2)])
                        for j in range(NJC):
                            mm = E_.matmul(ps_v[:, pb, :],
                                           xnT[:, j, tc * 128:(tc + 1) * 128],
                                           wv_sb[:, j, :],
                                           start=(j == 0), stop=(j == NJC - 1))
                        mm.then_inc(ts.h, 1)
                    ts.inc()
                    mm_done = ts.n
                    if eng == "tensor":
                        if MK.get(("lgr", tc - 2)) is not None:
                            E_.wait_ge(rt.h, MK[("lgr", tc - 2)])
                        for j in range(NJC):
                            mm = E_.matmul(ps_lg[:, pb, 0:E],
                                           xnT[:, j, tc * 128:(tc + 1) * 128],
                                           rwT[:, j, :], start=(j == 0),
                                           stop=(j == NJC - 1))
                        mm.then_inc(ts.h, 1)
                    ts.inc()
                    lg_done = ts.n
                    if eng == "scalar":
                        E_.wait_ge(ts.h, mm_done)
                        if MK.get(("b2sp", tc - 2)) is not None:
                            _s, _v = MK[("b2sp", tc - 2)]
                            E_.wait_ge(_s.h, _v)
                        else:
                            _s, _v = MK[("vsp", NTC - 2 + tc)]
                            E_.wait_ge(_s.h, _v)
                        E_.activation(v_ev[:, pb, :], ps_v[:, pb, :], AF.Copy).then_inc(st.h, 1)
                    st.inc()
                    MK[("b2ev", tc)] = st.n
                    _sps = SM["spA"] if pb == 0 else SM["spB"]
                    if eng == "sync":
                        E_.wait_ge(st.h, MK[("b2ev", tc)])
                        E_.dma_start(
                            v_scr[4:8, :, tc * 128:(tc + 1) * 128].transpose([1, 0, 2]),
                            v_ev[:, pb, :].rearrange("p (g d) -> p g d", d=128),
                        ).then_inc(_sps.h, 16)
                    _sps.inc(16)
                    MK[("b2sp", tc)] = (_sps, _sps.n)
                    if MK.get(("wdtr_pend",)) is not None:
                        flush_wdtr(MK.pop(("wdtr_pend",)))
                    if eng == "vector":
                        E_.wait_ge(ts.h, lg_done)
                    VHOP(E_.tensor_tensor(lg_sb[:, :], ps_lg[:, pb, 0:E], rb_sb[:, :], ALU.add) if eng == "vector" else None)
                    if eng == "vector":
                        E_.reduce_max(nlmax[:, :], lg_sb[:, :], AX.X, negate=True).then_inc(rt.h, 1)
                    rt.inc()
                    MK[("lgr", tc)] = rt.n
                    if eng == "scalar":
                        E_.wait_ge(rt.h, rt.n)
                        E_.activation(esb[:, :], lg_sb[:, :], AF.Exp, bias=nlmax[:, :],
                                      scale=1.0).then_inc(rt.h, 1)
                    rt.inc()
                    if eng == "vector":
                        E_.wait_ge(rt.h, rt.n)
                    VHOP(E_.reduce_sum(zz[:, :], esb[:, :], AX.X) if eng == "vector" else None)
                    if eng == "vector":
                        E_.reciprocal(rz1[:, :], zz[:, :]).then_inc(rt.h, 1)
                    rt.inc()
                    if eng == "scalar":
                        E_.wait_ge(rt.h, rt.n)
                        E_.activation(p_sb[:, :], esb[:, :], AF.Copy, scale=rz1[:, :]).then_inc(rt.h, 1)
                    rt.inc()
                    if eng == "vector":
                        E_.wait_ge(rt.h, rt.n)
                        if MK.get(("wdtr", tc - 2)) is not None:
                            E_.wait_ge(ts.h, MK[("wdtr", tc - 2)])
                    VHOP(E_.max(tk8[:, :], p_sb[:, :]) if eng == "vector" else None)
                    VHOP(E_.reduce_sum(wsum[:, tc:tc + 1], tk8[:, :], AX.X) if eng == "vector" else None)
                    if eng == "vector":
                        E_.scalar_tensor_tensor(wd_bf[:, pb, :], p_sb[:, :], tk8[:, 7:8], p_sb[:, :],
                                                ALU.is_ge, ALU.mult).then_inc(wd.h, 1)
                    wd.inc()
                    MK[("wdn", tc)] = wd.n
                    MK[("wdtr_pend",)] = tc
                if MK.get(("wdtr_pend",)) is not None:
                    flush_wdtr(MK.pop(("wdtr_pend",)))
                if eng == "vector":
                    E_.wait_ge(st.h, st.n)
                VHOP(E_.tensor_tensor(a1c[:, :], rstd[:, :], wsum[:, :], ALU.mult) if eng == "vector" else None)
                VHOP(E_.tensor_scalar(a1c[:, :], a1c[:, :], 1.0, None, ALU.add) if eng == "vector" else None)
                if eng == "vector":
                    E_.tensor_tensor(t16a[:, :], mrneg[:, :], wsum[:, :], ALU.mult).then_inc(vp.h, 1)
                vp.inc()
                if eng == "tensor":
                    E_.wait_ge(vp.h, vp.n)
                    E_.transpose(ps_lg[0:16, 0, 0:128], t16a[:, :],
                                 ident_f[:, :]).then_inc(ts.h, 1)
                ts.inc()
                if eng == "scalar":
                    E_.wait_ge(ts.h, ts.n)
                    E_.activation(t16T[:, :], ps_lg[0:16, 0, 0:128], AF.Copy).then_inc(st.h, 1)
                st.inc()
                if eng == "sync":
                    E_.wait_ge(st.h, st.n)
                    E_.dma_start(b1_scr[0:1, :].rearrange("o (tc t) -> o tc t", t=128),
                                 t16T[:, :]).then_inc(mr.h, 16)
                mr.inc(16)

            run_block(phaseB2)

        with contextlib.ExitStack() as phc:
            Ec = phc.enter_context
            ctab = Ec(nc.sbuf_tensor("ctab_sb", [128, S], F32))
            stab = Ec(nc.sbuf_tensor("stab_sb", [128, S], F32))
            qres = Ec(nc.sbuf_tensor("qres", [128, 512], F32))
            qshuf = Ec(nc.sbuf_tensor("qshuf", [128, 512], F32))
            tri01 = Ec(nc.sbuf_tensor("tri01", [128, 128], BF16))
            ones_c = Ec(nc.sbuf_tensor("ones_c", [128, 1], BF16))
            ones_r = Ec(nc.sbuf_tensor("ones_rC", [1, 128], F32))
            w_ring = Ec(nc.sbuf_tensor("w_ring", [128, 2, NJC, 128], BF16))
            qt_sb = Ec(nc.sbuf_tensor("qt_sb", [128, S], BF16))
            kt_sb = Ec(nc.sbuf_tensor("kt_sb", [128, S], BF16))
            vg_sb = Ec(nc.sbuf_tensor("vg_sb", [128, NTC, 128], BF16))
            pt_sb = Ec(nc.sbuf_tensor("pt_sb", [128, NPT, 512], BF16))
            rz5 = Ec(nc.sbuf_tensor("rz5", [1, 512], F32))
            rzb = Ec(nc.sbuf_tensor("rzb", [128, 512], F32))
            on_ev = Ec(nc.sbuf_tensor("on_ev", [128, 2, 512], BF16))
            ps_q = Ec(nc.psum_tensor("ps_q", [128, 2, 512], F32))
            ps_s = Ec(nc.psum_tensor("ps_s", [128, 2, 512], F32))
            ps_o = Ec(nc.psum_tensor("ps_o", [128, 2, 512], F32))
            ps_z = Ec(nc.psum_tensor("ps_z", [128, 512], F32))
            ps_rz = Ec(nc.psum_tensor("ps_rz", [128, 512], F32))

            def emit_qk_head(eng, E_, w_dram, head, buf, scaled, dst_sb, mk_key):
                qm, qe, sw, rp = SM["qm"], SM["qe"], SM["sw"], SM["rp"]
                wqs = SM["wqA"] if buf == 0 else SM["wqB"]
                if eng == "gpsimd":
                    prev = MK.get(("wbuf", buf))
                    if prev is not None:
                        E_.wait_ge(qm.h, prev)
                    for c in range(8):
                        E_.dma_start(
                            w_ring[:, buf, c * 4:(c + 1) * 4, :],
                            w_dram[c * 512:(c + 1) * 512, head * 128:(head + 1) * 128]
                            .rearrange("(jg p) i -> p jg i", p=128),
                        ).then_inc(wqs.h, 16)
                wqs.inc(16 * 8)
                w_ready = wqs.n
                for q in range(4):
                    bk = q % 2
                    if eng == "tensor":
                        if q == 0:
                            E_.wait_ge(wqs.h, w_ready)
                        if MK.get(("qe_bank", bk)) is not None:
                            E_.wait_ge(qe.h, MK[("qe_bank", bk)])
                        for j in range(NJC):
                            mm = E_.matmul(ps_q[:, bk, :], w_ring[:, buf, j, :],
                                           xnT[:, j, q * 512:(q + 1) * 512],
                                           start=(j == 0), stop=(j == NJC - 1))
                        mm.then_inc(qm.h, 1)
                    qm.inc()
                    if eng == "scalar":
                        E_.wait_ge(qm.h, qm.n)
                        prev_rope = MK.get(("rope_last",))
                        if prev_rope is not None:
                            E_.wait_ge(rp.h, prev_rope)
                        if scaled:
                            E_.activation(qres[:, :], ps_q[:, bk, :], AF.Copy,
                                          scale=rec[:, :]).then_inc(qe.h, 1)
                        else:
                            E_.activation(qres[:, :], ps_q[:, bk, :], AF.Copy).then_inc(qe.h, 1)
                    qe.inc()
                    MK[("qe_bank", bk)] = qe.n
                    if eng == "vector":
                        E_.wait_ge(qe.h, qe.n)
                        dep = MK.get(("dst_free", mk_key))
                        if dep is not None and q == 0:
                            E_.wait_ge(dep[0], dep[1])
                        col0 = q * 512
                        E_.stream_shuffle(qshuf[:, :], qres[:, :], SWAP_MASK)
                        E_.tensor_tensor(qres[:, :], qres[:, :], ctab[:, col0:col0 + 512], ALU.mult)
                        E_.tensor_tensor(qshuf[:, :], qshuf[:, :], stab[:, col0:col0 + 512], ALU.mult)
                        E_.tensor_tensor(dst_sb[:, col0:col0 + 512], qres[:, :], qshuf[:, :],
                                         ALU.add).then_inc(rp.h, 1)
                    rp.inc()
                    MK[("rope_last",)] = rp.n
                    MK[(mk_key, q)] = rp.n
                MK[("wbuf", buf)] = qm.n

            def flush_bc_tail(eng, E_):
                pend = MK.get(("bc_pending",))
                if pend is None:
                    return
                MK[("bc_pending",)] = None
                ph, pqtb, pob, poz = pend
                ts, st, onv, oz = SM["ts"], SM["st"], SM["onv"], SM["oz"]
                if eng == "tensor":
                    E_.wait_ge(oz.h, poz)
                    if MK.get(("rzb_ev",)) is not None:
                        E_.wait_ge(st.h, MK[("rzb_ev",)])
                    E_.matmul(ps_rz[:, :], ones_r[0:1, 0:128], rz5[:, :],
                              start=True, stop=True).then_inc(ts.h, 1)
                ts.inc()
                bc_done = ts.n
                if eng == "scalar":
                    E_.wait_ge(ts.h, bc_done)
                    E_.activation(rzb[:, :], ps_rz[:, :], AF.Copy).then_inc(st.h, 1)
                st.inc()
                MK[("rzb_ev",)] = st.n
                if eng == "vector":
                    E_.wait_ge(st.h, MK[("rzb_ev",)])
                    dep = MK.get(("onv_spill", pqtb % 2))
                    if dep is not None:
                        E_.wait_ge(dep[0].h, dep[1])
                    E_.tensor_tensor(on_ev[:, pqtb % 2, :], ps_o[:, pob, :], rzb[:, :],
                                     ALU.mult).then_inc(onv.h, 1)
                onv.inc()
                MK[("onv_prev",)] = onv.n
                MK[("onv_bank", pob)] = onv.n
                _cs = SM["spA"] if pqtb % 2 == 0 else SM["spB"]
                if eng == "sync":
                    E_.wait_ge(onv.h, MK[("onv_prev",)])
                    E_.dma_start(
                        ont_scr[pqtb * 4:(pqtb + 1) * 4, :, ph * 128:(ph + 1) * 128]
                        .transpose([1, 0, 2]),
                        on_ev[:, pqtb % 2, :].rearrange("p (tcq tt) -> p tcq tt", tt=128),
                    ).then_inc(_cs.h, 16)
                _cs.inc(16)
                MK[("onv_spill", pqtb % 2)] = (_cs, _cs.n)

            def emit_attention(eng, E_, h):
                ts, se, sm, pc, oz, onv, sp, st = (
                    SM["ts"], SM["se"], SM["sm"], SM["pc"], SM["oz"], SM["onv"],
                    SM["sp"], SM["st"])
                tile_ctr = SM["_tile"]
                for qtb in range(4):
                    ob = (h * 4 + qtb) % 2
                    nk = 4 * (qtb + 1)
                    flush_bc_tail(eng, E_)
                    tiles = []
                    for ktc in range(nk):
                        goff = max(0, ktc * 128 - qtb * 512)
                        npr = 512 - goff
                        slot = tile_ctr.n % NPT
                        tno = tile_ctr.inc()
                        diag = (ktc >= qtb * 4)
                        if eng == "tensor":
                            if ktc == 0:
                                E_.wait_ge(SM["rp"].h, MK[("q", qtb)])
                            if MK.get(("se_tile", tno - 2)) is not None:
                                E_.wait_ge(se.h, MK[("se_tile", tno - 2)])
                            E_.matmul(ps_s[:, tno % 2, goff:goff + npr],
                                      kt_sb[:, ktc * 128:(ktc + 1) * 128],
                                      qt_sb[:, qtb * 512 + goff:(qtb + 1) * 512],
                                      start=True, stop=True).then_inc(ts.h, 1)
                        ts.inc()
                        MK[("ts_tile", tno)] = ts.n
                        if eng == "scalar":
                            E_.wait_ge(ts.h, MK[("ts_tile", tno)])
                            if MK.get(("pc_tile", tno - NPT)) is not None:
                                E_.wait_ge(pc.h, MK[("pc_tile", tno - NPT)])
                            E_.activation(pt_sb[:, slot, goff:goff + npr],
                                          ps_s[:, tno % 2, goff:goff + npr],
                                          AF.Exp).then_inc(se.h, 1)
                        se.inc()
                        MK[("se_tile", tno)] = se.n
                        if diag:
                            if eng == "vector":
                                E_.wait_ge(se.h, MK[("se_tile", tno)])
                                E_.tensor_tensor(pt_sb[:, slot, goff:goff + 128],
                                                 pt_sb[:, slot, goff:goff + 128],
                                                 tri01[:, :], ALU.mult).then_inc(sm.h, 1)
                            sm.inc()
                            MK[("sm_tile", tno)] = sm.n
                        tiles.append((ktc, goff, npr, slot, tno, diag))
                    for i, (ktc, goff, npr, slot, tno, diag) in enumerate(tiles):
                        if eng == "tensor":
                            if diag:
                                E_.wait_ge(sm.h, MK[("sm_tile", tno)])
                            else:
                                E_.wait_ge(se.h, MK[("se_tile", tno)])
                            if i == 0 and MK.get(("oz_prev",)) is not None:
                                E_.wait_ge(oz.h, MK[("oz_prev",)])
                            mm = E_.matmul(ps_z[0:1, goff:goff + npr], ones_c[:, :],
                                           pt_sb[:, slot, goff:goff + npr],
                                           start=(i == 0), stop=(i == nk - 1))
                            if i == nk - 1:
                                mm.then_inc(ts.h, 1)
                        if i == nk - 1:
                            ts.inc()
                    z_done = ts.n
                    if eng == "vector":
                        E_.wait_ge(ts.h, z_done)
                        if MK.get(("rzb_ev",)) is not None:
                            E_.wait_ge(st.h, MK[("rzb_ev",)])
                        E_.reciprocal(rz5[:, :], ps_z[0:1, :]).then_inc(oz.h, 1)
                    oz.inc()
                    MK[("oz_prev",)] = oz.n
                    for i, (ktc, goff, npr, slot, tno, diag) in enumerate(tiles):
                        if eng == "tensor":
                            if i == 0 and MK.get(("onv_bank", ob)) is not None:
                                E_.wait_ge(onv.h, MK[("onv_bank", ob)])
                            E_.matmul(ps_o[:, ob, goff:goff + npr], vg_sb[:, ktc, :],
                                      pt_sb[:, slot, goff:goff + npr],
                                      start=(i == 0), stop=(i == nk - 1)).then_inc(pc.h, 1)
                        pc.inc()
                        MK[("pc_tile", tno)] = pc.n
                    MK[("bc_pending",)] = (h, qtb, ob, oz.n)

            def phaseC(eng, E_):
                ld, kvl, sp, rp, ts, onv = (SM["ld"], SM["kvl"], SM["sp"], SM["rp"],
                                            SM["ts"], SM["onv"])
                for dst, srcd in [(ctab, ctab_in), (stab, stab_in), (tri01, tri_in),
                                  (ones_c, onescol_in), (ones_r, onesrow_in[0:1, 0:128])]:
                    if eng == "sync":
                        E_.dma_start(dst[:, :], srcd[:, :]).then_inc(ld.h, 16)
                    ld.inc(16)
                tabs = ld.n
                if eng == "vector":
                    E_.wait_ge(ld.h, tabs)
                for kh in range(NKV):
                    if kh > 0:
                        MK[("dst_free", "k")] = (sp.h, MK[("ksp", kh - 1)])
                    emit_qk_head(eng, E_, wk, kh, kh % 2, False, kt_sb, "k")
                    if eng == "sync":
                        E_.wait_ge(rp.h, MK[("k", 3)])
                        E_.dma_start(kt_scr[kh, :, :], kt_sb[:, :]).then_inc(sp.h, 16)
                    sp.inc(16)
                    MK[("ksp", kh)] = sp.n
                for h in range(NQH):
                    g = h // 4
                    if h % 4 == 0:
                        if eng == "sync":
                            if h > 0:
                                E_.wait_ge(ts.h, MK[("grp_ts",)])
                                E_.wait_ge(onv.h, MK[("onv_prev",)])
                                E_.wait_ge(SM["pc"].h, SM["pc"].n)
                            else:
                                E_.wait_ge(sp.h, MK[("ksp", NKV - 1)])
                                for _s, _v in MK[("vsp_fin",)]:
                                    E_.wait_ge(_s.h, _v)
                            E_.dma_start(kt_sb[:, :], kt_scr[g, :, :]).then_inc(kvl.h, 16)
                            E_.dma_start(vg_sb[:, :, :],
                                         v_scr[g, :, :].rearrange("p (tcq tt) -> p tcq tt", tt=128),
                                         ).then_inc(kvl.h, 16)
                        kvl.inc(32)
                        MK[("kv_ready",)] = kvl.n
                    if h > 0:
                        MK[("dst_free", "q")] = (ts.h, MK[("qts_real",)])
                        if DEBUG:
                            MK[("dst_free", "q")] = (SM["od"].h, MK[("qdbg", h - 1)])
                    emit_qk_head(eng, E_, wq, h, h % 2, True, qt_sb, "q")
                    if DEBUG:
                        if eng == "sync":
                            E_.wait_ge(SM["rp"].h, MK[("q", 3)])
                            E_.dma_start(dbg_qt[h, :, :], qt_sb[:, :]).then_inc(SM["od"].h, 16)
                        SM["od"].inc(16)
                        MK[("qts_last",)] = None
                    if eng == "tensor":
                        E_.wait_ge(kvl.h, MK[("kv_ready",)])
                    emit_attention(eng, E_, h)
                    MK[("qts_real",)] = ts.n
                    MK[("qdbg", h)] = SM["od"].n
                    MK[("grp_ts",)] = ts.n
                flush_bc_tail(eng, E_)

            run_block(phaseC)
        _xnT_cm.close()

        with contextlib.ExitStack() as phd:
            Ed = phd.enter_context
            wo_sb = Ed(nc.sbuf_tensor("wo_sb", [128, NJC, 2048], BF16))
            eb_sb = Ed(nc.sbuf_tensor("eb_sb", [E, D], BF16))
            b1_sb = Ed(nc.sbuf_tensor("b1_sb", [1, S], F32))
            ones_r = Ed(nc.sbuf_tensor("ones_rD", [1, 512], F32))
            lhs_sb = Ed(nc.sbuf_tensor("lhs_sb", [128, 2, NJC, 128], BF16))
            xin_sb = Ed(nc.sbuf_tensor("xin_sb", [128, 2, 2048], F32))
            out_sb = Ed(nc.sbuf_tensor("out_sb", [128, 2, 2048], F32))
            ps_d = Ed(nc.psum_tensor("ps_d", [128, 2, 2048], F32))

            def phaseD(eng, E_):
                wol, lhs, xD, ts, onv, od, dc = (SM["wol"], SM["lhs"], SM["xD"],
                                                 SM["ts"], SM["onv"], SM["od"], SM["dc"])
                if eng == "gpsimd":
                    E_.dma_start(eb_sb[:, :], eb[:, :]).then_inc(dc.h, 16)
                dc.inc(16)
                if eng == "sync":
                    E_.dma_start(b1_sb[:, :], b1_scr[:, :]).then_inc(SM["mr"].h, 16)
                    E_.dma_start(ones_r[:, :], onesrow_in[:, :]).then_inc(SM["mr"].h, 16)
                SM["mr"].inc(32)
                dc_all = dc.n
                mr_all = SM["mr"].n
                for half in range(2):
                    wo_base = wol.n
                    for c in range(8):
                        if eng == "gpsimd":
                            if half == 1 and c == 0:
                                E_.wait_ge(ts.h, MK[("d_half_ts", 0)])
                            E_.dma_start(
                                wo_sb[:, c * 4:(c + 1) * 4, :],
                                wo[c * 512:(c + 1) * 512, half * 2048:(half + 1) * 2048]
                                .rearrange("(jg p) i -> p jg i", p=128),
                            ).then_inc(SM["wc%d" % c].h, 16)
                        SM["wc%d" % c].inc(16)
                    wo_ready = wol.n
                    for tc in range(NTC):
                        pb = tc % 2
                        gi = half * NTC + tc
                        _lh = SM["lhA"] if pb == 0 else SM["lhB"]
                        _xd = SM["xdA"] if pb == 0 else SM["xdB"]
                        _od = SM["odA"] if pb == 0 else SM["odB"]
                        if eng == "sync":
                            if MK.get(("d_mm", gi - 2)) is not None:
                                E_.wait_ge(ts.h, MK[("d_mm", gi - 2)])
                            if MK.get(("d_ev", gi - 2)) is not None:
                                E_.wait_ge(onv.h, MK[("d_ev", gi - 2)])
                            E_.dma_start(
                                lhs_sb[:, pb, :, :],
                                ont_scr[tc, :, :].rearrange("p (j i) -> p j i", i=128),
                            ).then_inc(_lh.h, 16)
                            E_.dma_start(
                                xin_sb[:, pb, :],
                                x_in[tc * 128:(tc + 1) * 128, half * 2048:(half + 1) * 2048],
                            ).then_inc(_xd.h, 16)
                        _lh.inc(16)
                        _xd.inc(16)
                        if eng == "tensor":
                            E_.wait_ge(_lh.h, _lh.n)
                            if tc == 0 and half == 0:
                                E_.wait_ge(dc.h, dc_all)
                                E_.wait_ge(SM["mr"].h, mr_all)
                            if MK.get(("d_ev", gi - 2)) is not None:
                                E_.wait_ge(onv.h, MK[("d_ev", gi - 2)])
                            for j in range(NJC):
                                if tc == 0 and j % 4 == 0:
                                    E_.wait_ge(SM["wc%d" % (j // 4)].h, 16 * (half + 1))
                                for c in range(4):
                                    E_.matmul(ps_d[:, pb, c * 512:(c + 1) * 512],
                                              lhs_sb[:, pb, j, :],
                                              wo_sb[:, j, c * 512:(c + 1) * 512],
                                              start=(j == 0), stop=False)
                            for c in range(4):
                                E_.matmul(ps_d[:, pb, c * 512:(c + 1) * 512],
                                          wdT[:, tc * 128:(tc + 1) * 128],
                                          eb_sb[:, half * 2048 + c * 512: half * 2048 + (c + 1) * 512],
                                          start=False, stop=False)
                            for c in range(4):
                                mm = E_.matmul(ps_d[:, pb, c * 512:(c + 1) * 512],
                                               b1_sb[0:1, tc * 128:(tc + 1) * 128], ones_r[:, :],
                                               start=False, stop=True)
                            mm.then_inc(ts.h, 1)
                        ts.inc()
                        MK[("d_mm", gi)] = ts.n
                        if eng == "vector":
                            E_.wait_ge(ts.h, MK[("d_mm", gi)])
                            E_.wait_ge(_xd.h, _xd.n)
                            if MK.get(("d_out", gi - 2)) is not None:
                                _o, _v = MK[("d_out", gi - 2)]
                                E_.wait_ge(_o.h, _v)
                            for c in range(4):
                                E_.scalar_tensor_tensor(
                                    out_sb[:, pb, c * 512:(c + 1) * 512],
                                    xin_sb[:, pb, c * 512:(c + 1) * 512],
                                    a1c[:, tc:tc + 1],
                                    ps_d[:, pb, c * 512:(c + 1) * 512],
                                    ALU.mult, ALU.add)
                            E_.engine_nop().then_inc(onv.h, 1)
                        onv.inc()
                        MK[("d_ev", gi)] = onv.n
                        if eng == "gpsimd":
                            E_.wait_ge(onv.h, MK[("d_ev", gi)])
                            E_.dma_start(
                                out[tc * 128:(tc + 1) * 128, half * 2048:(half + 1) * 2048],
                                out_sb[:, pb, :],
                            ).then_inc(_od.h, 16)
                        _od.inc(16)
                        MK[("d_out", gi)] = (_od, _od.n)
                    MK[("d_half_ts", half)] = ts.n
                if eng == "gpsimd":
                    E_.wait_ge(SM["odA"].h, SM["odA"].n)
                    E_.wait_ge(SM["odB"].h, SM["odB"].n)

            run_block(phaseD)

    return nc


def kernel(L0_attn_x, L0_attn_wq, L0_attn_wk, L0_attn_wv, L0_attn_wo,
           L0_ffn_router_weight, L0_ffn_router_bias, L0_ffn_expert_bias):
    C, Sg = _rope_tables()
    bf = ml_dtypes.bfloat16
    ident_b = np.eye(128, dtype=bf)
    ident_f = np.eye(128, dtype=np.float32)
    tri = np.triu(np.ones((128, 128), np.float32)).astype(bf)
    ones_col = np.ones((128, 1), bf)
    ones_row = np.ones((1, 512), np.float32)
    c32 = np.full((128, 1), np.float32(np.sqrt(np.float32(128.0))), np.float32)
    rb_bc = np.ascontiguousarray(np.broadcast_to(
        np.asarray(L0_ffn_router_bias, np.float32)[None, :], (128, E)))

    nc = build_program()
    x = np.ascontiguousarray(np.asarray(L0_attn_x, np.float32))
    common = {
        "wq": np.ascontiguousarray(L0_attn_wq), "wk": np.ascontiguousarray(L0_attn_wk),
        "wv": np.ascontiguousarray(L0_attn_wv), "wo": np.ascontiguousarray(L0_attn_wo),
        "rw": np.ascontiguousarray(L0_ffn_router_weight), "rb_bc": rb_bc,
        "eb": np.ascontiguousarray(L0_ffn_expert_bias),
        "ctab": C, "stab": Sg, "identb": ident_b, "identf": ident_f,
        "tri": tri, "onescol": ones_col, "onesrow": ones_row, "c32": c32,
    }
    in_maps = [dict(common, x=x[b]) for b in range(8)]
    res = run_bass_kernel_spmd(nc, in_maps, core_ids=list(range(8)))
    return np.stack([np.asarray(r["out"], np.float32) for r in res.results], axis=0)


# version 16 (final)
# speedup vs baseline: 1.2005x; 1.0712x over previous
import contextlib
import numpy as np
import ml_dtypes
import concourse.bass as bass
import concourse.mybir as mybir
from concourse.bass_utils import run_bass_kernel_spmd

F32 = mybir.dt.float32
BF16 = mybir.dt.bfloat16
AF = mybir.ActivationFunctionType
ALU = mybir.AluOpType
AX = mybir.AxisListType

S = 2048
D = 4096
NTC = 16
NJC = 32
NQH = 32
NKV = 8
E = 64
NPT = 17

ENGS = ("tensor", "scalar", "vector", "sync", "gpsimd")
SWAP_MASK = [i ^ 1 for i in range(32)]


class Sem:
    __slots__ = ("h", "n")

    def __init__(self, handle=None):
        self.h = handle
        self.n = 0

    def inc(self, k=1):
        self.n += k
        return self.n


def _rope_tables():
    import jax
    import jax.numpy as jnp
    cpu = jax.devices("cpu")[0]
    with jax.default_device(cpu):
        freqs = 1.0 / 10000.0 ** (jnp.arange(0, 128, 2, dtype=jnp.float32) / 128)
        t = jnp.arange(S, dtype=jnp.float32)
        ang = jnp.outer(t, freqs)
        cos = np.asarray(jnp.cos(ang))
        sin = np.asarray(jnp.sin(ang))
    C = np.empty((128, S), np.float32)
    Sg = np.empty((128, S), np.float32)
    C[0::2] = cos.T
    C[1::2] = cos.T
    Sg[0::2] = -sin.T
    Sg[1::2] = sin.T
    return np.ascontiguousarray(C), np.ascontiguousarray(Sg)


DEBUG = False


def build_program():
    nc = bass.Bass("TRN2", target_bir_lowering=False, detect_race_conditions=False)

    x_in = nc.dram_tensor("x", [S, D], F32, kind="ExternalInput")
    wq = nc.dram_tensor("wq", [D, D], F32, kind="ExternalInput")
    wk = nc.dram_tensor("wk", [D, NKV * 128], F32, kind="ExternalInput")
    wv = nc.dram_tensor("wv", [D, NKV * 128], F32, kind="ExternalInput")
    wo = nc.dram_tensor("wo", [D, D], F32, kind="ExternalInput")
    rw = nc.dram_tensor("rw", [E, D], F32, kind="ExternalInput")
    rb_bc = nc.dram_tensor("rb_bc", [128, E], F32, kind="ExternalInput")
    eb = nc.dram_tensor("eb", [E, D], F32, kind="ExternalInput")
    ctab_in = nc.dram_tensor("ctab", [128, S], F32, kind="ExternalInput")
    stab_in = nc.dram_tensor("stab", [128, S], F32, kind="ExternalInput")
    identb_in = nc.dram_tensor("identb", [128, 128], BF16, kind="ExternalInput")
    identf_in = nc.dram_tensor("identf", [128, 128], F32, kind="ExternalInput")
    tri_in = nc.dram_tensor("tri", [128, 128], BF16, kind="ExternalInput")
    onescol_in = nc.dram_tensor("onescol", [128, 1], BF16, kind="ExternalInput")
    onesrow_in = nc.dram_tensor("onesrow", [1, 512], F32, kind="ExternalInput")
    c32_in = nc.dram_tensor("c32", [128, 1], F32, kind="ExternalInput")
    out = nc.dram_tensor("out", [S, D], F32, kind="ExternalOutput")
    scr_kind = "ExternalOutput" if DEBUG else "Internal"
    kt_scr = nc.dram_tensor("kt_scr", [NKV, 128, S], BF16, kind=scr_kind)
    v_scr = nc.dram_tensor("v_scr", [NKV, 128, S], BF16, kind=scr_kind)
    ont_scr = nc.dram_tensor("ont_scr", [NTC, 128, D], BF16, kind=scr_kind)
    b1_scr = nc.dram_tensor("b1_scr", [1, S], F32, kind=scr_kind)
    if DEBUG:
        dbg_xnT = nc.dram_tensor("dbg_xnT", [128, NJC, S], BF16, kind="ExternalOutput")
        dbg_wdT = nc.dram_tensor("dbg_wdT", [E, S], BF16, kind="ExternalOutput")
        dbg_st = nc.dram_tensor("dbg_st", [128, 4 * NTC], F32, kind="ExternalOutput")
        dbg_st2 = nc.dram_tensor("dbg_st2", [128, 2 * NTC], F32, kind="ExternalOutput")
        dbg_qt = nc.dram_tensor("dbg_qt", [NQH, 128, S], BF16, kind="ExternalOutput")

    with contextlib.ExitStack() as top:
        Et = top.enter_context
        sem_names = [
            "ld", "x1", "x2", "st", "ts", "vp", "ve", "rt", "wd", "wvl",
            "wql", "qm", "qe", "sw", "rp", "kvl", "se", "sm", "pc", "oz",
            "onv", "sp", "wol", "lhs", "xD", "dc", "od", "mr",
            "wc0", "wc1", "wc2", "wc3", "wc4", "wc5", "wc6", "wc7",
            "wqA", "wqB", "spA", "spB", "lhA", "lhB", "xdA", "xdB", "odA", "odB",
            "vsA", "vsB",
        ]
        SM = {n: Sem(Et(nc.semaphore(n))) for n in sem_names}
        SM["_tile"] = Sem()
        MK = {}

        def VH(eng, E_, ins):
            sw = SM["sw"]
            sw.inc()
            if eng == "vector":
                ins.then_inc(sw.h, 1)
                E_.wait_ge(sw.h, sw.n)

        def run_block(phase_fn):
            snap = {k: s.n for k, s in SM.items()}
            mk_snap = dict(MK)
            with nc.Block() as blk:
                for name in ENGS:
                    def walk(E_, name=name):
                        for k, s in SM.items():
                            s.n = snap[k]
                        MK.clear()
                        MK.update(mk_snap)
                        phase_fn(name, E_)
                    getattr(blk, name)(walk)

        c32_sb = Et(nc.sbuf_tensor("c32_sb", [128, 1], F32))
        rec = Et(nc.sbuf_tensor("rec", [128, 1], F32))
        wdT = Et(nc.sbuf_tensor("wdT", [E, S], BF16))
        rstd = Et(nc.sbuf_tensor("rstd", [128, NTC], F32))
        mrneg = Et(nc.sbuf_tensor("mrneg", [128, NTC], F32))
        wsum = Et(nc.sbuf_tensor("wsum", [128, NTC], F32))
        a1c = Et(nc.sbuf_tensor("a1c", [128, NTC], F32))
        t16a = Et(nc.sbuf_tensor("t16a", [128, NTC], F32))
        _xnT_cm = contextlib.ExitStack()
        xnT = _xnT_cm.enter_context(nc.sbuf_tensor("xnT", [128, NJC, S], BF16))

        with contextlib.ExitStack() as phab:
            Ea = phab.enter_context
            ident_f = Ea(nc.sbuf_tensor("ident_f", [128, 128], F32))
            ident_b = Ea(nc.sbuf_tensor("ident_b", [128, 128], BF16))
            rb_sb = Ea(nc.sbuf_tensor("rb_sb", [128, E], F32))
            x2h = Ea(nc.sbuf_tensor("x2h", [128, 2, 2048], F32))
            th_sb = Ea(nc.sbuf_tensor("th_sb", [128, 2048], F32))
            t16T = th_sb[0:16, 1536:1664]
            xn_bf = Ea(nc.sbuf_tensor("xn_bf", [128, 2048], BF16))
            rw_bf = Ea(nc.sbuf_tensor("rw_bf", [E, D], BF16))
            rwT = Ea(nc.sbuf_tensor("rwT", [128, NJC, E], BF16))
            wv_sb = Ea(nc.sbuf_tensor("wv_sb", [128, NJC, 512], BF16))
            v_ev = Ea(nc.sbuf_tensor("v_ev", [128, 2, 512], BF16))
            rz1 = Ea(nc.sbuf_tensor("rz1", [128, 1], F32))
            s1a = Ea(nc.sbuf_tensor("s1a", [128, 1], F32))
            s1b = Ea(nc.sbuf_tensor("s1b", [128, 1], F32))
            s1c = Ea(nc.sbuf_tensor("s1c", [128, 1], F32))
            lg_sb = th_sb[:, 1792:1856]
            esb = th_sb[:, 1856:1920]
            p_sb = th_sb[:, 1920:1984]
            nlmax = th_sb[:, 1984:1985]
            zz = th_sb[:, 1985:1986]
            tk8 = th_sb[:, 1986:1994]
            wd_bf = Ea(nc.sbuf_tensor("wdb_sb", [128, 2, 64], BF16))
            ps_tb = Ea(nc.psum_tensor("ps_tb", [128, 2, 1024], BF16))
            ps_v = Ea(nc.psum_tensor("ps_v", [128, 2, 512], F32))
            ps_lg = Ea(nc.psum_tensor("ps_lg", [128, 2, 512], F32))
            ps_z0 = Ea(nc.psum_tensor("ps_z0", [128, 512], F32))

            def phaseAB(eng, E_):
                ld, st, ts, vp, ve, rt, wd, wvl, dc = (
                    SM["ld"], SM["st"], SM["ts"], SM["vp"], SM["ve"], SM["rt"],
                    SM["wd"], SM["wvl"], SM["dc"])
                lhA, lhB = SM["lhA"], SM["lhB"]
                mr = SM["mr"]

                def VHOP(ins):
                    sw = SM["sw"]
                    sw.inc()
                    if eng == "vector":
                        ins.then_inc(sw.h, 1)
                        E_.wait_ge(sw.h, sw.n)

                for dst, srcd in [(ident_f, identf_in), (ident_b, identb_in),
                                  (c32_sb, c32_in), (rb_sb, rb_bc)]:
                    if eng == "sync":
                        E_.dma_start(dst[:, :], srcd[:, :]).then_inc(ld.h, 16)
                    ld.inc(16)
                ld_all = ld.n
                if eng == "gpsimd":
                    E_.dma_start(rw_bf[:, :], rw[:, :]).then_inc(dc.h, 16)
                dc.inc(16)
                rw_done = dc.n
                for c in range(8):
                    if eng == "gpsimd":
                        E_.dma_start(
                            wv_sb[:, c * 4:(c + 1) * 4, :],
                            wv[c * 512:(c + 1) * 512, 0:512]
                            .rearrange("(jg p) i -> p jg i", p=128),
                        ).then_inc(wvl.h, 16)
                    wvl.inc(16)
                wv_all = wvl.n
                if eng == "vector":
                    E_.wait_ge(ld.h, ld_all)
                    E_.reciprocal(rec[:, :], c32_sb[:, :])
                    E_.memset(ps_z0[:, :], 0.0)
                    E_.engine_nop().then_inc(ve.h, 1)
                ve.inc()
                z0_done = ve.n
                for j in range(NJC):
                    bk = j % 2
                    if eng == "tensor":
                        if j == 0:
                            E_.wait_ge(dc.h, rw_done)
                            E_.wait_ge(ld.h, ld_all)
                        if j >= 2:
                            E_.wait_ge(st.h, st.n - 1)
                        E_.transpose(ps_tb[:, bk, 0:64], rw_bf[0:64, j * 128:(j + 1) * 128],
                                     ident_b[0:64, 0:64]).then_inc(ts.h, 1)
                    ts.inc()
                    if eng == "scalar":
                        E_.wait_ge(ts.h, ts.n)
                        E_.copy(rwT[:, j, :], ps_tb[:, bk, 0:64]).then_inc(st.h, 1)
                    st.inc()
                def flush_vev(ptc):
                    ppb = ptc % 2
                    _sps = SM["vsA"] if ppb == 0 else SM["vsB"]
                    if eng == "scalar":
                        E_.wait_ge(ts.h, MK[("vmm", ptc)])
                        if MK.get(("vsp", ptc - 2)) is not None:
                            _s, _v = MK[("vsp", ptc - 2)]
                            E_.wait_ge(_s.h, _v)
                        E_.activation(v_ev[:, ppb, :], ps_v[:, ppb, :],
                                      AF.Copy).then_inc(st.h, 1)
                    st.inc()
                    MK[("vev", ptc)] = st.n
                    if eng == "scalar":
                        E_.wait_ge(st.h, MK[("vev", ptc)])
                        E_.dma_start(
                            v_scr[0:4, :, ptc * 128:(ptc + 1) * 128].transpose([1, 0, 2]),
                            v_ev[:, ppb, :].rearrange("p (g d) -> p g d", d=128),
                        ).then_inc(_sps.h, 16)
                    _sps.inc(16)
                    MK[("vsp", ptc)] = (_sps, _sps.n)

                for tc in range(NTC):
                    pb = tc % 2
                    for hf, hsem in ((0, lhA), (1, lhB)):
                        if eng == "sync":
                            dep = MK.get(("xfree", hf))
                            if dep is not None:
                                E_.wait_ge(dep[0].h, dep[1])
                            E_.dma_start(
                                x2h[:, hf, :],
                                x_in[tc * 128:(tc + 1) * 128, hf * 2048:(hf + 1) * 2048],
                            ).then_inc(hsem.h, 16)
                        hsem.inc(16)
                        MK[("xload", hf)] = hsem.n
                    for hf, hsem, acc in ((0, lhA, s1a), (1, lhB, s1b)):
                        if eng == "scalar":
                            E_.wait_ge(hsem.h, MK[("xload", hf)])
                            E_.activation(th_sb[:, :], x2h[:, hf, :], AF.Square,
                                          accum_out=acc[:, :]).then_inc(rt.h, 1)
                        rt.inc()
                        MK[("sq", hf)] = rt.n
                        if eng == "vector":
                            E_.wait_ge(hsem.h, MK[("xload", hf)])
                            E_.reduce_sum(s1c[:, :] if hf == 0 else rz1[:, :],
                                          x2h[:, hf, :], AX.X).then_inc(ve.h, 1)
                        ve.inc()
                    if eng == "vector":
                        E_.wait_ge(rt.h, MK[("sq", 1)])
                    VHOP(E_.tensor_tensor(s1a[:, :], s1a[:, :], s1b[:, :], ALU.add) if eng == "vector" else None)
                    VHOP(E_.tensor_scalar(s1a[:, :], s1a[:, :], 1.0 / 4096, None, ALU.mult) if eng == "vector" else None)
                    VHOP(E_.tensor_tensor(s1c[:, :], s1c[:, :], rz1[:, :], ALU.add) if eng == "vector" else None)
                    VHOP(E_.tensor_scalar(s1c[:, :], s1c[:, :], 1.0 / 4096, None, ALU.mult) if eng == "vector" else None)
                    VHOP(E_.tensor_tensor(s1b[:, :], s1c[:, :], s1c[:, :], ALU.mult) if eng == "vector" else None)
                    VHOP(E_.tensor_tensor(s1a[:, :], s1a[:, :], s1b[:, :], ALU.subtract) if eng == "vector" else None)
                    if eng == "vector":
                        E_.tensor_scalar(s1a[:, :], s1a[:, :], 1e-5, None, ALU.add).then_inc(vp.h, 1)
                    vp.inc()
                    if eng == "scalar":
                        E_.wait_ge(vp.h, vp.n)
                        E_.activation(s1b[:, :], s1a[:, :], AF.Ln)
                        E_.activation(rz1[:, :], s1b[:, :], AF.Exp, scale=-0.5).then_inc(vp.h, 1)
                    vp.inc()
                    if eng == "vector":
                        E_.wait_ge(vp.h, vp.n)
                    for _ in range(2):
                        VHOP(E_.tensor_tensor(s1b[:, :], rz1[:, :], rz1[:, :], ALU.mult) if eng == "vector" else None)
                        VHOP(E_.tensor_tensor(s1b[:, :], s1b[:, :], s1a[:, :], ALU.mult) if eng == "vector" else None)
                        VHOP(E_.tensor_scalar(s1b[:, :], s1b[:, :], -0.5, 1.5, ALU.mult, ALU.add) if eng == "vector" else None)
                        VHOP(E_.tensor_tensor(rz1[:, :], rz1[:, :], s1b[:, :], ALU.mult) if eng == "vector" else None)
                    VHOP(E_.tensor_copy(rstd[:, tc:tc + 1], rz1[:, :]) if eng == "vector" else None)
                    VHOP(E_.tensor_tensor(s1c[:, :], s1c[:, :], rz1[:, :], ALU.mult) if eng == "vector" else None)
                    if eng == "vector":
                        E_.tensor_scalar(mrneg[:, tc:tc + 1], s1c[:, :], -1.0, None,
                                         ALU.mult).then_inc(ve.h, 1)
                    ve.inc()
                    stats_done = ve.n
                    for hf, hsem in ((0, lhA), (1, lhB)):
                        for sl in range(4):
                            if eng == "vector":
                                if hf == 0 and sl == 0:
                                    E_.wait_ge(ve.h, z0_done)
                                if hf == 1 and sl == 0:
                                    E_.wait_ge(ts.h, MK.get(("trA",), 0))
                                E_.scalar_tensor_tensor(
                                    th_sb[:, sl * 512:(sl + 1) * 512],
                                    x2h[:, hf, sl * 512:(sl + 1) * 512],
                                    mrneg[:, tc:tc + 1], ps_z0[:, :],
                                    ALU.add, ALU.add).then_inc(ve.h, 1)
                            ve.inc()
                            MK[("stt", hf, sl)] = ve.n
                            if eng == "scalar":
                                E_.wait_ge(ve.h, MK[("stt", hf, sl)])
                                if hf == 0 and sl == 0 and tc > 0:
                                    E_.wait_ge(ts.h, MK[("trB",)])
                                E_.activation(xn_bf[:, sl * 512:(sl + 1) * 512],
                                              th_sb[:, sl * 512:(sl + 1) * 512],
                                              AF.Copy, scale=rstd[:, tc:tc + 1]).then_inc(rt.h, 1)
                            rt.inc()
                        MK[("xnact", hf)] = rt.n
                        if hf == 1:
                            MK[("sttB",)] = ve.n
                        MK[("xfree", hf)] = (SM["ve"], MK[("stt", hf, 3)])
                        for g in range(4):
                            bk = g % 2
                            if eng == "tensor":
                                if g == 0:
                                    E_.wait_ge(rt.h, MK[("xnact", hf)])
                                E_.wait_ge(st.h, st.n - 1)
                                for q in range(4):
                                    jj = g * 4 + q
                                    mm = E_.transpose(ps_tb[:, bk, q * 128:(q + 1) * 128],
                                                      xn_bf[:, jj * 128:(jj + 1) * 128],
                                                      ident_b[:, :])
                                mm.then_inc(ts.h, 1)
                            ts.inc()
                            if eng == "scalar":
                                E_.wait_ge(ts.h, ts.n)
                                E_.copy(xnT[:, hf * 16 + g * 4:hf * 16 + (g + 1) * 4,
                                            tc * 128:(tc + 1) * 128],
                                        ps_tb[:, bk, 0:512].rearrange("p (b c) -> p b c", c=128)
                                        ).then_inc(st.h, 1)
                            st.inc()
                        if hf == 0:
                            MK[("trA",)] = ts.n
                        else:
                            MK[("trB",)] = ts.n
                    MK[("xn_done",)] = st.n
                    if MK.get(("vev_pend",)) is not None:
                        flush_vev(MK.pop(("vev_pend",)))
                    if eng == "tensor":
                        if tc == 0:
                            E_.wait_ge(wvl.h, wv_all)
                        E_.wait_ge(st.h, MK[("xn_done",)])
                        if MK.get(("vev", tc - 2)) is not None:
                            E_.wait_ge(st.h, MK[("vev", tc - 2)])
                        for j in range(NJC):
                            mm = E_.matmul(ps_v[:, pb, :],
                                           xnT[:, j, tc * 128:(tc + 1) * 128],
                                           wv_sb[:, j, :],
                                           start=(j == 0), stop=(j == NJC - 1))
                        mm.then_inc(ts.h, 1)
                    ts.inc()
                    MK[("vmm", tc)] = ts.n
                    MK[("vev_pend",)] = tc
                if MK.get(("vev_pend",)) is not None:
                    flush_vev(MK.pop(("vev_pend",)))
                MK[("vsp_fin",)] = [(SM["vsA"], SM["vsA"].n),
                                    (SM["vsB"], SM["vsB"].n)]

            run_block(phaseAB)

            def phaseB2(eng, E_):
                wvl, ts, st, rt, wd, vp, mr = (SM["wvl"], SM["ts"], SM["st"],
                                               SM["rt"], SM["wd"], SM["vp"],
                                               SM["mr"])

                def VHOP(ins):
                    sw = SM["sw"]
                    sw.inc()
                    if eng == "vector":
                        ins.then_inc(sw.h, 1)
                        E_.wait_ge(sw.h, sw.n)

                for c in range(8):
                    if eng == "gpsimd":
                        E_.dma_start(
                            wv_sb[:, c * 4:(c + 1) * 4, :],
                            wv[c * 512:(c + 1) * 512, 512:1024]
                            .rearrange("(jg p) i -> p jg i", p=128),
                        ).then_inc(wvl.h, 16)
                    wvl.inc(16)
                wv_all = wvl.n

                def flush_wdtr(ptc):
                    if eng == "tensor":
                        E_.wait_ge(wd.h, MK[("wdn", ptc)])
                        E_.wait_ge(st.h, st.n - 1)
                        E_.transpose(ps_tb[0:64, (ptc + 1) % 2, 0:128],
                                     wd_bf[:, ptc % 2, :],
                                     ident_b[:, :]).then_inc(ts.h, 1)
                    ts.inc()
                    MK[("wdtr", ptc)] = ts.n
                    if eng == "scalar":
                        E_.wait_ge(ts.h, MK[("wdtr", ptc)])
                        E_.copy(wdT[:, ptc * 128:(ptc + 1) * 128],
                                ps_tb[0:64, (ptc + 1) % 2, 0:128]).then_inc(st.h, 1)
                    st.inc()

                for tc in range(NTC):
                    pb = tc % 2
                    if eng == "tensor":
                        if MK.get(("b2ev", tc - 2)) is not None:
                            E_.wait_ge(st.h, MK[("b2ev", tc - 2)])
                        for j in range(NJC):
                            if tc == 0 and j % 4 == 0:
                                E_.wait_ge(wvl.h, wv_all - 128 + 16 * (j // 4 + 1))
                            mm = E_.matmul(ps_v[:, pb, :],
                                           xnT[:, j, tc * 128:(tc + 1) * 128],
                                           wv_sb[:, j, :],
                                           start=(j == 0), stop=(j == NJC - 1))
                        mm.then_inc(ts.h, 1)
                    ts.inc()
                    mm_done = ts.n
                    if eng == "tensor":
                        if MK.get(("lgr", tc - 2)) is not None:
                            E_.wait_ge(rt.h, MK[("lgr", tc - 2)])
                        for j in range(NJC):
                            mm = E_.matmul(ps_lg[:, pb, 0:E],
                                           xnT[:, j, tc * 128:(tc + 1) * 128],
                                           rwT[:, j, :], start=(j == 0),
                                           stop=(j == NJC - 1))
                        mm.then_inc(ts.h, 1)
                    ts.inc()
                    lg_done = ts.n
                    if eng == "scalar":
                        E_.wait_ge(ts.h, mm_done)
                        if MK.get(("b2sp", tc - 2)) is not None:
                            _s, _v = MK[("b2sp", tc - 2)]
                            E_.wait_ge(_s.h, _v)
                        else:
                            _s, _v = MK[("vsp", NTC - 2 + tc)]
                            E_.wait_ge(_s.h, _v)
                        E_.activation(v_ev[:, pb, :], ps_v[:, pb, :], AF.Copy).then_inc(st.h, 1)
                    st.inc()
                    MK[("b2ev", tc)] = st.n
                    _sps = SM["spA"] if pb == 0 else SM["spB"]
                    if eng == "sync":
                        E_.wait_ge(st.h, MK[("b2ev", tc)])
                        E_.dma_start(
                            v_scr[4:8, :, tc * 128:(tc + 1) * 128].transpose([1, 0, 2]),
                            v_ev[:, pb, :].rearrange("p (g d) -> p g d", d=128),
                        ).then_inc(_sps.h, 16)
                    _sps.inc(16)
                    MK[("b2sp", tc)] = (_sps, _sps.n)
                    if MK.get(("wdtr_pend",)) is not None:
                        flush_wdtr(MK.pop(("wdtr_pend",)))
                    if eng == "vector":
                        E_.wait_ge(ts.h, lg_done)
                    VHOP(E_.tensor_tensor(lg_sb[:, :], ps_lg[:, pb, 0:E], rb_sb[:, :], ALU.add) if eng == "vector" else None)
                    if eng == "vector":
                        E_.reduce_max(nlmax[:, :], lg_sb[:, :], AX.X, negate=True).then_inc(rt.h, 1)
                    rt.inc()
                    MK[("lgr", tc)] = rt.n
                    if eng == "scalar":
                        E_.wait_ge(rt.h, rt.n)
                        E_.activation(esb[:, :], lg_sb[:, :], AF.Exp, bias=nlmax[:, :],
                                      scale=1.0).then_inc(rt.h, 1)
                    rt.inc()
                    if eng == "vector":
                        E_.wait_ge(rt.h, rt.n)
                    VHOP(E_.reduce_sum(zz[:, :], esb[:, :], AX.X) if eng == "vector" else None)
                    if eng == "vector":
                        E_.reciprocal(rz1[:, :], zz[:, :]).then_inc(rt.h, 1)
                    rt.inc()
                    if eng == "scalar":
                        E_.wait_ge(rt.h, rt.n)
                        E_.activation(p_sb[:, :], esb[:, :], AF.Copy, scale=rz1[:, :]).then_inc(rt.h, 1)
                    rt.inc()
                    if eng == "vector":
                        E_.wait_ge(rt.h, rt.n)
                        if MK.get(("wdtr", tc - 2)) is not None:
                            E_.wait_ge(ts.h, MK[("wdtr", tc - 2)])
                    VHOP(E_.max(tk8[:, :], p_sb[:, :]) if eng == "vector" else None)
                    VHOP(E_.reduce_sum(wsum[:, tc:tc + 1], tk8[:, :], AX.X) if eng == "vector" else None)
                    if eng == "vector":
                        E_.scalar_tensor_tensor(wd_bf[:, pb, :], p_sb[:, :], tk8[:, 7:8], p_sb[:, :],
                                                ALU.is_ge, ALU.mult).then_inc(wd.h, 1)
                    wd.inc()
                    MK[("wdn", tc)] = wd.n
                    MK[("wdtr_pend",)] = tc
                if MK.get(("wdtr_pend",)) is not None:
                    flush_wdtr(MK.pop(("wdtr_pend",)))
                if eng == "vector":
                    E_.wait_ge(st.h, st.n)
                VHOP(E_.tensor_tensor(a1c[:, :], rstd[:, :], wsum[:, :], ALU.mult) if eng == "vector" else None)
                VHOP(E_.tensor_scalar(a1c[:, :], a1c[:, :], 1.0, None, ALU.add) if eng == "vector" else None)
                if eng == "vector":
                    E_.tensor_tensor(t16a[:, :], mrneg[:, :], wsum[:, :], ALU.mult).then_inc(vp.h, 1)
                vp.inc()
                if eng == "tensor":
                    E_.wait_ge(vp.h, vp.n)
                    E_.transpose(ps_lg[0:16, 0, 0:128], t16a[:, :],
                                 ident_f[:, :]).then_inc(ts.h, 1)
                ts.inc()
                if eng == "scalar":
                    E_.wait_ge(ts.h, ts.n)
                    E_.activation(t16T[:, :], ps_lg[0:16, 0, 0:128], AF.Copy).then_inc(st.h, 1)
                st.inc()
                if eng == "sync":
                    E_.wait_ge(st.h, st.n)
                    E_.dma_start(b1_scr[0:1, :].rearrange("o (tc t) -> o tc t", t=128),
                                 t16T[:, :]).then_inc(mr.h, 16)
                mr.inc(16)

            run_block(phaseB2)

        with contextlib.ExitStack() as phc:
            Ec = phc.enter_context
            ctab = Ec(nc.sbuf_tensor("ctab_sb", [128, S], F32))
            stab = Ec(nc.sbuf_tensor("stab_sb", [128, S], F32))
            qres = Ec(nc.sbuf_tensor("qres", [128, 512], F32))
            qshuf = Ec(nc.sbuf_tensor("qshuf", [128, 512], F32))
            tri01 = Ec(nc.sbuf_tensor("tri01", [128, 128], BF16))
            ones_c = Ec(nc.sbuf_tensor("ones_c", [128, 1], BF16))
            ones_r = Ec(nc.sbuf_tensor("ones_rC", [1, 128], F32))
            w_ring = Ec(nc.sbuf_tensor("w_ring", [128, 2, NJC, 128], BF16))
            qt_sb = Ec(nc.sbuf_tensor("qt_sb", [128, S], BF16))
            kt_sb = Ec(nc.sbuf_tensor("kt_sb", [128, S], BF16))
            vg_sb = Ec(nc.sbuf_tensor("vg_sb", [128, NTC, 128], BF16))
            pt_sb = Ec(nc.sbuf_tensor("pt_sb", [128, NPT, 512], BF16))
            rz5 = Ec(nc.sbuf_tensor("rz5", [1, 512], F32))
            rzb = Ec(nc.sbuf_tensor("rzb", [128, 512], F32))
            on_ev = Ec(nc.sbuf_tensor("on_ev", [128, 2, 512], BF16))
            ps_q = Ec(nc.psum_tensor("ps_q", [128, 2, 512], F32))
            ps_s = Ec(nc.psum_tensor("ps_s", [128, 2, 512], F32))
            ps_o = Ec(nc.psum_tensor("ps_o", [128, 2, 512], F32))
            ps_z = Ec(nc.psum_tensor("ps_z", [128, 512], F32))
            ps_rz = Ec(nc.psum_tensor("ps_rz", [128, 512], F32))

            def emit_qk_head(eng, E_, w_dram, head, buf, scaled, dst_sb, mk_key):
                qm, qe, sw, rp = SM["qm"], SM["qe"], SM["sw"], SM["rp"]
                wqs = SM["wqA"] if buf == 0 else SM["wqB"]
                if eng == "gpsimd":
                    prev = MK.get(("wbuf", buf))
                    if prev is not None:
                        E_.wait_ge(qm.h, prev)
                    for c in range(8):
                        E_.dma_start(
                            w_ring[:, buf, c * 4:(c + 1) * 4, :],
                            w_dram[c * 512:(c + 1) * 512, head * 128:(head + 1) * 128]
                            .rearrange("(jg p) i -> p jg i", p=128),
                        ).then_inc(wqs.h, 16)
                wqs.inc(16 * 8)
                w_ready = wqs.n
                for q in range(4):
                    bk = q % 2
                    if eng == "tensor":
                        if q == 0:
                            E_.wait_ge(wqs.h, w_ready)
                        if MK.get(("qe_bank", bk)) is not None:
                            E_.wait_ge(qe.h, MK[("qe_bank", bk)])
                        if MK.get(("psq_exp", bk)) is not None:
                            E_.wait_ge(SM["se"].h, MK[("psq_exp", bk)])
                        for j in range(NJC):
                            mm = E_.matmul(ps_q[:, bk, :], w_ring[:, buf, j, :],
                                           xnT[:, j, q * 512:(q + 1) * 512],
                                           start=(j == 0), stop=(j == NJC - 1))
                        mm.then_inc(qm.h, 1)
                    qm.inc()
                    if eng == "scalar":
                        E_.wait_ge(qm.h, qm.n)
                        prev_rope = MK.get(("rope_last",))
                        if prev_rope is not None:
                            E_.wait_ge(rp.h, prev_rope)
                        if scaled:
                            E_.activation(qres[:, :], ps_q[:, bk, :], AF.Copy,
                                          scale=rec[:, :]).then_inc(qe.h, 1)
                        else:
                            E_.activation(qres[:, :], ps_q[:, bk, :], AF.Copy).then_inc(qe.h, 1)
                    qe.inc()
                    MK[("qe_bank", bk)] = qe.n
                    if eng == "vector":
                        E_.wait_ge(qe.h, qe.n)
                        dep = MK.get(("dst_free", mk_key))
                        if dep is not None and q == 0:
                            E_.wait_ge(dep[0], dep[1])
                        col0 = q * 512
                        E_.stream_shuffle(qshuf[:, :], qres[:, :], SWAP_MASK)
                        E_.tensor_tensor(qres[:, :], qres[:, :], ctab[:, col0:col0 + 512], ALU.mult)
                        E_.tensor_tensor(qshuf[:, :], qshuf[:, :], stab[:, col0:col0 + 512], ALU.mult)
                        E_.tensor_tensor(dst_sb[:, col0:col0 + 512], qres[:, :], qshuf[:, :],
                                         ALU.add).then_inc(rp.h, 1)
                    rp.inc()
                    MK[("rope_last",)] = rp.n
                    MK[(mk_key, q)] = rp.n
                MK[("wbuf", buf)] = qm.n

            def flush_bc_tail(eng, E_):
                pend = MK.get(("bc_pending",))
                if pend is None:
                    return
                MK[("bc_pending",)] = None
                ph, pqtb, pob, poz = pend
                ts, st, onv, oz = SM["ts"], SM["st"], SM["onv"], SM["oz"]
                if eng == "tensor":
                    E_.wait_ge(oz.h, poz)
                    if MK.get(("rzb_ev",)) is not None:
                        E_.wait_ge(st.h, MK[("rzb_ev",)])
                    E_.matmul(ps_rz[:, :], ones_r[0:1, 0:128], rz5[:, :],
                              start=True, stop=True).then_inc(ts.h, 1)
                ts.inc()
                bc_done = ts.n
                if eng == "scalar":
                    E_.wait_ge(ts.h, bc_done)
                    E_.activation(rzb[:, :], ps_rz[:, :], AF.Copy).then_inc(st.h, 1)
                st.inc()
                MK[("rzb_ev",)] = st.n
                if eng == "vector":
                    E_.wait_ge(st.h, MK[("rzb_ev",)])
                    dep = MK.get(("onv_spill", pqtb % 2))
                    if dep is not None:
                        E_.wait_ge(dep[0].h, dep[1])
                    E_.tensor_tensor(on_ev[:, pqtb % 2, :], ps_o[:, pob, :], rzb[:, :],
                                     ALU.mult).then_inc(onv.h, 1)
                onv.inc()
                MK[("onv_prev",)] = onv.n
                MK[("onv_bank", pob)] = onv.n
                _cs = SM["spA"] if pqtb % 2 == 0 else SM["spB"]
                if eng == "sync":
                    E_.wait_ge(onv.h, MK[("onv_prev",)])
                    E_.dma_start(
                        ont_scr[pqtb * 4:(pqtb + 1) * 4, :, ph * 128:(ph + 1) * 128]
                        .transpose([1, 0, 2]),
                        on_ev[:, pqtb % 2, :].rearrange("p (tcq tt) -> p tcq tt", tt=128),
                    ).then_inc(_cs.h, 16)
                _cs.inc(16)
                MK[("onv_spill", pqtb % 2)] = (_cs, _cs.n)

            def emit_attention(eng, E_, h):
                ts, se, sm, pc, oz, onv, sp, st = (
                    SM["ts"], SM["se"], SM["sm"], SM["pc"], SM["oz"], SM["onv"],
                    SM["sp"], SM["st"])
                tile_ctr = SM["_tile"]
                for qtb in range(4):
                    ob = (h * 4 + qtb) % 2
                    nk = 4 * (qtb + 1)
                    flush_bc_tail(eng, E_)
                    tiles = []
                    for ktc in range(nk):
                        goff = max(0, ktc * 128 - qtb * 512)
                        npr = 512 - goff
                        slot = tile_ctr.n % NPT
                        tno = tile_ctr.inc()
                        diag = (ktc >= qtb * 4)
                        bsel = tno % 4
                        if bsel < 2:
                            sc_ps = ps_s[:, bsel, goff:goff + npr]
                        else:
                            sc_ps = ps_q[:, bsel - 2, goff:goff + npr]
                        if eng == "tensor":
                            if ktc == 0:
                                E_.wait_ge(SM["rp"].h, MK[("q", qtb)])
                            if bsel >= 2 and MK.get(("qe_bank", bsel - 2)) is not None:
                                E_.wait_ge(SM["qe"].h, MK[("qe_bank", bsel - 2)])
                            if MK.get(("se_tile", tno - 4)) is not None:
                                E_.wait_ge(se.h, MK[("se_tile", tno - 4)])
                            E_.matmul(sc_ps,
                                      kt_sb[:, ktc * 128:(ktc + 1) * 128],
                                      qt_sb[:, qtb * 512 + goff:(qtb + 1) * 512],
                                      start=True, stop=True).then_inc(ts.h, 1)
                        ts.inc()
                        MK[("ts_tile", tno)] = ts.n
                        if eng == "scalar":
                            E_.wait_ge(ts.h, MK[("ts_tile", tno)])
                            if MK.get(("pc_tile", tno - NPT)) is not None:
                                E_.wait_ge(pc.h, MK[("pc_tile", tno - NPT)])
                            E_.activation(pt_sb[:, slot, goff:goff + npr],
                                          sc_ps,
                                          AF.Exp).then_inc(se.h, 1)
                        se.inc()
                        MK[("se_tile", tno)] = se.n
                        if bsel >= 2:
                            MK[("psq_exp", bsel - 2)] = se.n
                        if diag:
                            if eng == "vector":
                                E_.wait_ge(se.h, MK[("se_tile", tno)])
                                E_.tensor_tensor(pt_sb[:, slot, goff:goff + 128],
                                                 pt_sb[:, slot, goff:goff + 128],
                                                 tri01[:, :], ALU.mult).then_inc(sm.h, 1)
                            sm.inc()
                            MK[("sm_tile", tno)] = sm.n
                        tiles.append((ktc, goff, npr, slot, tno, diag))
                    for i, (ktc, goff, npr, slot, tno, diag) in enumerate(tiles):
                        if eng == "tensor":
                            if diag:
                                E_.wait_ge(sm.h, MK[("sm_tile", tno)])
                            else:
                                E_.wait_ge(se.h, MK[("se_tile", tno)])
                            if i == 0 and MK.get(("oz_prev",)) is not None:
                                E_.wait_ge(oz.h, MK[("oz_prev",)])
                            mm = E_.matmul(ps_z[0:1, goff:goff + npr], ones_c[:, :],
                                           pt_sb[:, slot, goff:goff + npr],
                                           start=(i == 0), stop=(i == nk - 1))
                            if i == nk - 1:
                                mm.then_inc(ts.h, 1)
                        if i == nk - 1:
                            ts.inc()
                    z_done = ts.n
                    if eng == "vector":
                        E_.wait_ge(ts.h, z_done)
                        if MK.get(("rzb_ev",)) is not None:
                            E_.wait_ge(st.h, MK[("rzb_ev",)])
                        E_.reciprocal(rz5[:, :], ps_z[0:1, :]).then_inc(oz.h, 1)
                    oz.inc()
                    MK[("oz_prev",)] = oz.n
                    for i, (ktc, goff, npr, slot, tno, diag) in enumerate(tiles):
                        if eng == "tensor":
                            if i == 0 and MK.get(("onv_bank", ob)) is not None:
                                E_.wait_ge(onv.h, MK[("onv_bank", ob)])
                            E_.matmul(ps_o[:, ob, goff:goff + npr], vg_sb[:, ktc, :],
                                      pt_sb[:, slot, goff:goff + npr],
                                      start=(i == 0), stop=(i == nk - 1)).then_inc(pc.h, 1)
                        pc.inc()
                        MK[("pc_tile", tno)] = pc.n
                    MK[("bc_pending",)] = (h, qtb, ob, oz.n)

            def phaseC(eng, E_):
                ld, kvl, sp, rp, ts, onv = (SM["ld"], SM["kvl"], SM["sp"], SM["rp"],
                                            SM["ts"], SM["onv"])
                for dst, srcd in [(ctab, ctab_in), (stab, stab_in), (tri01, tri_in),
                                  (ones_c, onescol_in), (ones_r, onesrow_in[0:1, 0:128])]:
                    if eng == "sync":
                        E_.dma_start(dst[:, :], srcd[:, :]).then_inc(ld.h, 16)
                    ld.inc(16)
                tabs = ld.n
                if eng == "vector":
                    E_.wait_ge(ld.h, tabs)
                for kh in range(NKV):
                    if kh > 0:
                        MK[("dst_free", "k")] = (sp.h, MK[("ksp", kh - 1)])
                    emit_qk_head(eng, E_, wk, kh, kh % 2, False, kt_sb, "k")
                    if eng == "sync":
                        E_.wait_ge(rp.h, MK[("k", 3)])
                        E_.dma_start(kt_scr[kh, :, :], kt_sb[:, :]).then_inc(sp.h, 16)
                    sp.inc(16)
                    MK[("ksp", kh)] = sp.n
                for h in range(NQH):
                    g = h // 4
                    if h % 4 == 0:
                        if eng == "sync":
                            if h > 0:
                                E_.wait_ge(ts.h, MK[("grp_ts",)])
                                E_.wait_ge(onv.h, MK[("onv_prev",)])
                                E_.wait_ge(SM["pc"].h, SM["pc"].n)
                            else:
                                E_.wait_ge(sp.h, MK[("ksp", NKV - 1)])
                                for _s, _v in MK[("vsp_fin",)]:
                                    E_.wait_ge(_s.h, _v)
                            E_.dma_start(kt_sb[:, :], kt_scr[g, :, :]).then_inc(kvl.h, 16)
                            E_.dma_start(vg_sb[:, :, :],
                                         v_scr[g, :, :].rearrange("p (tcq tt) -> p tcq tt", tt=128),
                                         ).then_inc(kvl.h, 16)
                        kvl.inc(32)
                        MK[("kv_ready",)] = kvl.n
                    if h > 0:
                        MK[("dst_free", "q")] = (ts.h, MK[("qts_real",)])
                        if DEBUG:
                            MK[("dst_free", "q")] = (SM["od"].h, MK[("qdbg", h - 1)])
                    emit_qk_head(eng, E_, wq, h, h % 2, True, qt_sb, "q")
                    if DEBUG:
                        if eng == "sync":
                            E_.wait_ge(SM["rp"].h, MK[("q", 3)])
                            E_.dma_start(dbg_qt[h, :, :], qt_sb[:, :]).then_inc(SM["od"].h, 16)
                        SM["od"].inc(16)
                        MK[("qts_last",)] = None
                    if eng == "tensor":
                        E_.wait_ge(kvl.h, MK[("kv_ready",)])
                    emit_attention(eng, E_, h)
                    MK[("qts_real",)] = ts.n
                    MK[("qdbg", h)] = SM["od"].n
                    MK[("grp_ts",)] = ts.n
                flush_bc_tail(eng, E_)

            run_block(phaseC)
        _xnT_cm.close()

        with contextlib.ExitStack() as phd:
            Ed = phd.enter_context
            wo_sb = Ed(nc.sbuf_tensor("wo_sb", [128, NJC, 2048], BF16))
            eb_sb = Ed(nc.sbuf_tensor("eb_sb", [E, D], BF16))
            b1_sb = Ed(nc.sbuf_tensor("b1_sb", [1, S], F32))
            ones_r = Ed(nc.sbuf_tensor("ones_rD", [1, 512], F32))
            lhs_sb = Ed(nc.sbuf_tensor("lhs_sb", [128, 2, NJC, 128], BF16))
            xin_sb = Ed(nc.sbuf_tensor("xin_sb", [128, 2, 2048], F32))
            out_sb = Ed(nc.sbuf_tensor("out_sb", [128, 2, 2048], F32))
            ps_d = Ed(nc.psum_tensor("ps_d", [128, 2, 2048], F32))

            def phaseD(eng, E_):
                wol, lhs, xD, ts, onv, od, dc = (SM["wol"], SM["lhs"], SM["xD"],
                                                 SM["ts"], SM["onv"], SM["od"], SM["dc"])
                if eng == "gpsimd":
                    E_.dma_start(eb_sb[:, :], eb[:, :]).then_inc(dc.h, 16)
                dc.inc(16)
                if eng == "sync":
                    E_.dma_start(b1_sb[:, :], b1_scr[:, :]).then_inc(SM["mr"].h, 16)
                    E_.dma_start(ones_r[:, :], onesrow_in[:, :]).then_inc(SM["mr"].h, 16)
                SM["mr"].inc(32)
                dc_all = dc.n
                mr_all = SM["mr"].n
                for half in range(2):
                    wo_base = wol.n
                    for c in range(8):
                        if eng == "gpsimd":
                            if half == 1 and c == 0:
                                E_.wait_ge(ts.h, MK[("d_half_ts", 0)])
                            E_.dma_start(
                                wo_sb[:, c * 4:(c + 1) * 4, :],
                                wo[c * 512:(c + 1) * 512, half * 2048:(half + 1) * 2048]
                                .rearrange("(jg p) i -> p jg i", p=128),
                            ).then_inc(SM["wc%d" % c].h, 16)
                        SM["wc%d" % c].inc(16)
                    wo_ready = wol.n
                    for tc in range(NTC):
                        pb = tc % 2
                        gi = half * NTC + tc
                        _lh = SM["lhA"] if pb == 0 else SM["lhB"]
                        _xd = SM["xdA"] if pb == 0 else SM["xdB"]
                        _od = SM["odA"] if pb == 0 else SM["odB"]
                        if eng == "sync":
                            if MK.get(("d_mm", gi - 2)) is not None:
                                E_.wait_ge(ts.h, MK[("d_mm", gi - 2)])
                            if MK.get(("d_ev", gi - 2)) is not None:
                                E_.wait_ge(onv.h, MK[("d_ev", gi - 2)])
                            E_.dma_start(
                                lhs_sb[:, pb, :, :],
                                ont_scr[tc, :, :].rearrange("p (j i) -> p j i", i=128),
                            ).then_inc(_lh.h, 16)
                            E_.dma_start(
                                xin_sb[:, pb, :],
                                x_in[tc * 128:(tc + 1) * 128, half * 2048:(half + 1) * 2048],
                            ).then_inc(_xd.h, 16)
                        _lh.inc(16)
                        _xd.inc(16)
                        if eng == "tensor":
                            E_.wait_ge(_lh.h, _lh.n)
                            if tc == 0 and half == 0:
                                E_.wait_ge(dc.h, dc_all)
                                E_.wait_ge(SM["mr"].h, mr_all)
                            if MK.get(("d_ev", gi - 2)) is not None:
                                E_.wait_ge(onv.h, MK[("d_ev", gi - 2)])
                            for j in range(NJC):
                                if tc == 0 and j % 4 == 0:
                                    E_.wait_ge(SM["wc%d" % (j // 4)].h, 16 * (half + 1))
                                for c in range(4):
                                    E_.matmul(ps_d[:, pb, c * 512:(c + 1) * 512],
                                              lhs_sb[:, pb, j, :],
                                              wo_sb[:, j, c * 512:(c + 1) * 512],
                                              start=(j == 0), stop=False)
                            for c in range(4):
                                E_.matmul(ps_d[:, pb, c * 512:(c + 1) * 512],
                                          wdT[:, tc * 128:(tc + 1) * 128],
                                          eb_sb[:, half * 2048 + c * 512: half * 2048 + (c + 1) * 512],
                                          start=False, stop=False)
                            for c in range(4):
                                mm = E_.matmul(ps_d[:, pb, c * 512:(c + 1) * 512],
                                               b1_sb[0:1, tc * 128:(tc + 1) * 128], ones_r[:, :],
                                               start=False, stop=True)
                            mm.then_inc(ts.h, 1)
                        ts.inc()
                        MK[("d_mm", gi)] = ts.n
                        if eng == "vector":
                            E_.wait_ge(ts.h, MK[("d_mm", gi)])
                            E_.wait_ge(_xd.h, _xd.n)
                            if MK.get(("d_out", gi - 2)) is not None:
                                _o, _v = MK[("d_out", gi - 2)]
                                E_.wait_ge(_o.h, _v)
                            for c in range(4):
                                E_.scalar_tensor_tensor(
                                    out_sb[:, pb, c * 512:(c + 1) * 512],
                                    xin_sb[:, pb, c * 512:(c + 1) * 512],
                                    a1c[:, tc:tc + 1],
                                    ps_d[:, pb, c * 512:(c + 1) * 512],
                                    ALU.mult, ALU.add)
                            E_.engine_nop().then_inc(onv.h, 1)
                        onv.inc()
                        MK[("d_ev", gi)] = onv.n
                        if eng == "gpsimd":
                            E_.wait_ge(onv.h, MK[("d_ev", gi)])
                            E_.dma_start(
                                out[tc * 128:(tc + 1) * 128, half * 2048:(half + 1) * 2048],
                                out_sb[:, pb, :],
                            ).then_inc(_od.h, 16)
                        _od.inc(16)
                        MK[("d_out", gi)] = (_od, _od.n)
                    MK[("d_half_ts", half)] = ts.n
                if eng == "gpsimd":
                    E_.wait_ge(SM["odA"].h, SM["odA"].n)
                    E_.wait_ge(SM["odB"].h, SM["odB"].n)

            run_block(phaseD)

    return nc


def kernel(L0_attn_x, L0_attn_wq, L0_attn_wk, L0_attn_wv, L0_attn_wo,
           L0_ffn_router_weight, L0_ffn_router_bias, L0_ffn_expert_bias):
    C, Sg = _rope_tables()
    bf = ml_dtypes.bfloat16
    ident_b = np.eye(128, dtype=bf)
    ident_f = np.eye(128, dtype=np.float32)
    tri = np.triu(np.ones((128, 128), np.float32)).astype(bf)
    ones_col = np.ones((128, 1), bf)
    ones_row = np.ones((1, 512), np.float32)
    c32 = np.full((128, 1), np.float32(np.sqrt(np.float32(128.0))), np.float32)
    rb_bc = np.ascontiguousarray(np.broadcast_to(
        np.asarray(L0_ffn_router_bias, np.float32)[None, :], (128, E)))

    nc = build_program()
    x = np.ascontiguousarray(np.asarray(L0_attn_x, np.float32))
    common = {
        "wq": np.ascontiguousarray(L0_attn_wq), "wk": np.ascontiguousarray(L0_attn_wk),
        "wv": np.ascontiguousarray(L0_attn_wv), "wo": np.ascontiguousarray(L0_attn_wo),
        "rw": np.ascontiguousarray(L0_ffn_router_weight), "rb_bc": rb_bc,
        "eb": np.ascontiguousarray(L0_ffn_expert_bias),
        "ctab": C, "stab": Sg, "identb": ident_b, "identf": ident_f,
        "tri": tri, "onescol": ones_col, "onesrow": ones_row, "c32": c32,
    }
    in_maps = [dict(common, x=x[b]) for b in range(8)]
    res = run_bass_kernel_spmd(nc, in_maps, core_ids=list(range(8)))
    return np.stack([np.asarray(r["out"], np.float32) for r in res.results], axis=0)
